# version 14
# baseline (speedup 1.0000x reference)
import numpy as np

import concourse.bass as bass
import concourse.mybir as mybir
from concourse.alu_op_type import AluOpType as Op
from concourse.bass_utils import run_bass_kernel_spmd

F32 = mybir.dt.float32
F32R = mybir.dt.float32r
BF16 = mybir.dt.bfloat16
AF = mybir.ActivationFunctionType
AX = mybir.AxisListType.X

B = 8
T = 2048
DM = 1024
DI = 2048
DS = 16
DC = 4
FF = 4096
HN, HD = 16, 64
NE = 8
FFN = 4096
NDM, NDI, NF, NFN = DM // 128, DI // 128, FF // 128, FFN // 128
TC = 512
NCH = T // TC
NTT = TC // 128
LN_EPS = 1e-5
LN2 = float(np.log(2.0))

ENGS = ("sync", "tensor", "scalar", "vector", "gpsimd")
DMA_ENG = "sync"


class _Op:
    __slots__ = ("i", "eng", "fn", "r", "w", "dma", "sem", "val", "deps", "signal",
                 "fsig", "_bar_deps")

    def __init__(self, i, eng, fn, r, w, dma, sem, fsig=False):
        self.i, self.eng, self.fn, self.r, self.w, self.dma, self.sem = i, eng, fn, r, w, dma, sem
        self.val = None
        self.deps = []
        self.signal = False
        self.fsig = fsig


class Prog:

    def __init__(self, nc):
        self.nc = nc
        self.all = []
        self.per_eng = {e: [] for e in ENGS}
        self.fence_op = None

    def op(self, eng, fn, r=(), w=(), dma=False, dmasem=None, fsig=False, swdge=False):
        assert eng in ENGS
        if dma and eng == "gpsimd" and not swdge:
            eng = DMA_ENG
        r = tuple(r) + ("const",)
        sem = None
        if dma:
            assert dmasem is not None, "dma op needs dmasem key"
            sem = "D:" + dmasem
        o = _Op(len(self.all), eng, fn, tuple(r), tuple(w), dma, sem, fsig)
        self.all.append(o)
        self.per_eng[eng].append(o)
        return o

    def barrier(self):
        self.fence_op = ("BAR", None)
        o = self.op("gpsimd", None, r=("__bar__",), w=("__bar__",))
        o.fn = "__BARRIER__"
        return o

    def plan(self, es, sim_safe=True):
        last_writer = {}
        readers = {}
        bar_pending = None

        def rkey(o):
            return ("D", o.sem) if o.dma else ("E", o.eng)

        for o in self.all:
            deps = set()
            if o.fn == "__BARRIER__":
                bar_pending = o
                o._bar_deps = [q for q in self.all[:o.i]]
                continue
            if bar_pending is not None:
                deps.add(bar_pending)
            for rr in o.r:
                wN = last_writer.get(rr)
                if wN is not None:
                    deps.add(wN)
            for ww in o.w:
                wN = last_writer.get(ww)
                if wN is not None:
                    deps.add(wN)
                for rd in readers.get(ww, {}).values():
                    deps.add(rd)
            for ww in o.w:
                last_writer[ww] = o
                readers[ww] = {}
            for rr in o.r:
                readers.setdefault(rr, {})[rkey(o)] = o
            o.deps = [d for d in deps if d is not o]
        for o in self.all:
            if o.fn == "__BARRIER__":
                per_last = {}
                dma_deps = []
                for q in o._bar_deps:
                    if q.dma:
                        dma_deps.append(q)
                    else:
                        per_last[q.eng] = q
                o.deps = list(per_last.values()) + dma_deps
        for o in self.all:
            if o.fsig:
                o.signal = True
            for d in o.deps:
                if sim_safe or d.dma or d.eng != o.eng or d.fsig:
                    d.signal = True
        sems = {}
        vals = {}

        def sem_of(o):
            if o.dma:
                return o.sem
            return "E:" + o.eng

        for o in self.all:
            if o.dma or o.signal or o.fn == "__BARRIER__":
                s = sem_of(o)
                if s not in sems:
                    sems[s] = es.enter_context(self.nc.semaphore(s.replace(":", "_")))
                    vals[s] = 0
                vals[s] += 16 if o.dma else 1
                o.val = (s, vals[s])
                if o.dma:
                    o.signal = True
        cum = {}
        cum_at = []
        snap = {}
        for o in self.all:
            cum_at.append(dict(snap) if o in () else None)
        run = {}
        idx_counts = [None] * len(self.all)
        for o in self.all:
            idx_counts[o.i] = dict(run) if any(
                (d.dma for d in o.deps)) or o.fn == "__BARRIER__" else None
            if o.dma:
                run[o.sem] = run.get(o.sem, 0) + 16
        self.dma_wait = idx_counts
        self.sems = sems
        nsig = sum(1 for o in self.all if o.signal)
        return len(sems), nsig

    def emit(self, block):
        def make(eng_name):
            def body(eng):
                seen = {}
                for o in self.per_eng[eng_name]:
                    for d in o.deps:
                        if d.val is None:
                            continue
                        if d.dma:
                            s = d.sem
                            v = self.dma_wait[o.i].get(s, 0)
                            if v <= 0:
                                continue
                        else:
                            s, v = d.val
                        if seen.get(s, -1) >= v:
                            continue
                        eng.wait_ge(self.sems[s], v)
                        seen[s] = v
                    if o.fn == "__BARRIER__":
                        if o.val is not None:
                            eng.sem_inc(self.sems[o.val[0]], 1)
                        continue
                    inst = o.fn(eng)
                    if o.signal:
                        assert inst is not None, f"signaled op on {eng_name} returned None"
                        inst.then_inc(self.sems[o.val[0]], 16 if o.dma else 1)
            return body

        for e in ENGS:
            getattr(block, e)(make(e))


def build_program(debug=False, sim_safe=False, max_ops=None, mutate=None):
    from contextlib import ExitStack
    es = ExitStack()
    nc = bass.Bass("TRN2", target_bir_lowering=False)
    p = Prog(nc)

    def din(name, shape):
        return nc.dram_tensor(name, list(shape), F32, kind="ExternalInput").ap()

    x_own = din("x_own", (T, DM))
    x_full = din("x_full", (B * T, DM))
    ln_g = din("ln_gamma", (DM,))
    ln_b = din("ln_beta", (DM,))
    W_in = din("W_in", (FF, DM))
    conv_w = din("conv_w", (DI, DC))
    conv_b = din("conv_b", (DI,))
    W_x = din("W_x", (1 + 2 * DS, DI))
    w_dt = din("w_dt", (DI, 1))
    b_dt = din("b_dt", (DI,))
    A_log = din("A_log", (DI, DS))
    D_par = din("D_param", (DI,))
    W_out = din("W_out", (DM, DI))
    f0_w = din("ffn_w", (DM, DM))
    f0_b = din("ffn_b", (DM,))
    Wq_h = din("Wq", (DM, DM))
    bq_h = din("bq", (DM,))
    Wk_h = din("Wk", (DM, DM))
    bk_h = din("bk", (DM,))
    Wv_h = din("Wv", (DM, DM))
    bv_h = din("bv", (DM,))
    Wr_h = din("Wr", (DM, NE))
    br_h = din("br", (NE,))
    W1_h = din("W1", (NE, DM, FFN))
    W2_h = din("W2", (NE, FFN, DM))
    out_h = nc.dram_tensor("out", [T, DM], F32, kind="ExternalOutput").ap()

    def dsc(name, shape, dt=F32):
        return nc.dram_tensor(name, list(shape), dt, kind="Internal").ap()

    s_WinT = dsc("s_WinT", (NF, 128, DM))
    s_WoutT = dsc("s_WoutT", (NDM, 128, DI))
    s_WqT = dsc("s_WqT", (NDM, 128, DM))
    s_WkT = dsc("s_WkT", (NDM, 128, DM))
    s_WvT = dsc("s_WvT", (NDM, 128, DM))
    s_Wf0T = dsc("s_Wf0T", (NDM, 128, DM))
    s_dt = dsc("s_dt", (NDI, 128, TC))
    s_xc = dsc("s_xc", (NDI, 128, TC))
    s_sz = dsc("s_sz", (NDI, 128, TC))
    s_x1 = dsc("s_x1", (NDM, 128, T))
    s_xn1 = dsc("s_xn1", (NDM, 128, T))
    s_moe = dsc("s_moe", (NDM, 128, T))
    s_ctxt = dsc("s_ctxt", (T // 128, 128, DM))
    s_W1b = dsc("s_W1b", (NE, 128, NFN * DM), BF16)
    s_W2b = dsc("s_W2b", (NE, 128, 32 * DM), BF16)

    dbg = {}
    if debug:
        for nm, shp in [("d_xn", (NDM, 128, T)), ("d_xn2", (NDM, 128, T)),
                        ("d_xq", (NDM, 128, T)), ("d_f0", (NDM, 128, T)),
                        ("d_xc", (NDI, 128, T)), ("d_dt", (NDI, 128, T)),
                        ("d_sz", (NDI, 128, T)), ("d_y2", (NDI, 128, T)),
                        ("d_x1", (NDM, 128, T)), ("d_xn1", (NDM, 128, T)),
                        ("d_sc", (1, 8)),
                        ("d_q", (NDM, 128, T)), ("d_k", (NDM, 128, T)),
                        ("d_v", (NDM, 128, T)),
                        ("d_ctx", (T // 128, 128, DM)),
                        ("d_moe", (NDM, 128, T)), ("d_rt", (NE, T)),
                        ("d_msk", (NE, T)), ("d_mp", (128, T // 128))]:
            dbg[nm] = nc.dram_tensor(nm, list(shp), F32, kind="ExternalOutput").ap()

    ident_h = nc.inline_tensor(np.eye(128, dtype=np.float32), name="c_ident").ap()
    ones_h = nc.inline_tensor(np.ones((1, 128), dtype=np.float32), name="c_onesr").ap()
    onesc_h = nc.inline_tensor(np.ones((128, 1), dtype=np.float32), name="c_onesc").ap()
    BIGK = 1e30
    CONSTS = [0.0, 1.0, -1.0, 0.5, float(LN_EPS), 1.0 / DM, 1.0 / LN2, -6.0, -30.0,
              127.0, 1.0 / 127.0, 1e-8, 1e-12, 448.0, 2.0 ** -30, 0.125, 1e30,
              LN2, -LN2, 3.0 * LN2, -3.0 * LN2, 7.0 * LN2, -7.0 * LN2,
              -9.0 * LN2, 9.0 * LN2, -31.0 * LN2, 31.0 * LN2]
    CONSTS += [-BIGK * float(c) for c in
               [1, 2, 4, 8, 16, 32, 64, 128, 256, 512]]
    consts_h = nc.inline_tensor(np.tile(np.array(CONSTS, dtype=np.float32), (128, 1)),
                                name="c_vals").ap()
    sel_np = np.zeros((DS, DS * 128), dtype=np.float32)
    for _n in range(DS):
        sel_np[_n, _n * 128:(_n + 1) * 128] = 1.0
    sel_h = nc.inline_tensor(sel_np, name="c_sel").ap()

    def sb(name, shape, stack=es, dt=F32):
        return stack.enter_context(nc.sbuf_tensor(name, list(shape), dt))

    def pst(name, shape, stack=es):
        return stack.enter_context(nc.psum_tensor(name, list(shape), F32))

    sb_id = sb("b_id", (128, 128))
    sb_on = sb("b_on", (1, 128))
    sb_onc = sb("b_onc", (128, 1))
    sb_cn = sb("b_cn", (128, len(CONSTS)))

    def C(v):
        return sb_cn[:, CONSTS.index(v):CONSTS.index(v) + 1]

    for _i, _v in enumerate(CONSTS):
        nc.const_aps.register(F32, _v, sb_cn[:, _i:_i + 1])

    sb_gam = sb("b_gam", (128, NDM))
    sb_bet = sb("b_bet", (128, NDM))
    sb_cw = sb("b_cw", (128, NDI * DC))
    sb_cb = sb("b_cb", (128, NDI))
    sb_bdt = sb("b_bdt", (128, NDI))
    sb_A = sb("b_A", (128, NDI * DS))
    sb_D = sb("b_D", (128, NDI))
    sb_f0b = sb("b_f0b", (128, NDM))
    sb_bqkv = sb("b_bqkv", (128, 3 * NDM))
    sb_br = sb("b_br", (NE, 1))
    sb_wdtr = sb("b_wdtr", (1, DI))
    sb_WxT = sb("b_WxT", (128, NDI * 33))
    sb_Wr = sb("b_Wr", (128, NDM * NE))
    sb_gs = sb("b_gs", (128, 8))
    sb_msk = sb("b_mskf", (NE, T))
    sb_mpt = sb("b_mpt", (128, T // 128))
    sb_z8 = sb("b_z8", (128, NE))
    sb_sel = sb("b_sel", (DS, DS * 128))
    sb_jnk = sb("b_jnk", (128, DM))

    psA = pst("psA", (128, 1024))
    psB = pst("psB", (128, 1024))
    psC = pst("psC", (128, 512))
    psD = pst("psD", (128, 512))
    psE = pst("psE", (128, 512))
    psF = pst("psF", (128, 512))

    def R32(ap):
        return ap.bitcast(F32R)

    def MMF32(out, lhsT, rhs, start, stop):
        return lambda e: e.matmul(out, lhsT, rhs, start=start, stop=stop)

    def MM(out, lhsT, rhs, start, stop):
        return lambda e: e.matmul(out, lhsT.bitcast(F32R), rhs.bitcast(F32R),
                                  start=start, stop=stop)

    def MMB(out, lhsT, rhs, start, stop):
        return lambda e: e.matmul(out, lhsT, rhs, start=start, stop=stop)

    def TRA(out, in_, start=True, stop=True):
        np_ = in_.shape[0]
        b = in_.base_partition()
        return lambda e: e.matmul(out, in_, sb_id[b:b + np_, b:b + np_],
                                  start=start, stop=stop, is_transpose=True)

    def ACT(out, in_, func, scale=None, bias=None, accum=None):
        kw = {}
        if scale is not None:
            kw["scale"] = scale
        if bias is not None:
            kw["bias"] = bias
        if accum is not None:
            kw["accum_out"] = accum
        return lambda e: e.activation(out=out, in_=in_, func=func, **kw)

    def TS(out, in0, s1, op0, s2=None, op1=None):
        if op1 is None:
            return lambda e: e.tensor_scalar(out=out, in0=in0, scalar1=s1,
                                             scalar2=None, op0=op0)
        return lambda e: e.tensor_scalar(out=out, in0=in0, scalar1=s1, scalar2=s2,
                                         op0=op0, op1=op1)

    def TT(out, in0, in1, op):
        return lambda e: e.tensor_tensor(out=out, in0=in0, in1=in1, op=op)

    def STT(out, in0, s, in1, op0, op1):
        return lambda e: e.scalar_tensor_tensor(out=out, in0=in0, scalar=s, in1=in1,
                                                op0=op0, op1=op1)

    def SCAN(out, a, b, init):
        return lambda e: e.tensor_tensor_scan(out=out, data0=a, data1=b, initial=init,
                                              op0=Op.mult, op1=Op.add)

    def CUMSUM(out, a, zeros, init):
        return lambda e: e.tensor_tensor_scan(out=out, data0=a, data1=zeros, initial=init,
                                              op0=Op.add, op1=Op.add)

    def CPY(out, in_):
        return lambda e: e.tensor_copy(out=out, in_=in_)

    def RMAX(out, in_, absv=False):
        return lambda e: e.reduce_max(out=out, in_=in_, axis=AX,
                                      apply_absolute_value=absv)

    def REC(out, in_):
        return lambda e: e.reciprocal(out=out, in_=in_)

    def MEMS(ap, v):
        return lambda e: e.memset(ap, v)

    def CMPGE(f_out, r_in, c, rr, wr):
        p.op("scalar", ACT(f_out, r_in, AF.Sigmoid, scale=C(BIGK), bias=C(-BIGK * float(c))),
             r=rr + ("const",), w=wr)

    def quant_grid(src, out, wA, wB, wC, wD, mant, eclamp, scale_col=None,
                   lnbias_col=None, rr=(), wr=(), out_r32=False):
        o, rA, rB, rC, rD = wr
        o = o if isinstance(o, tuple) else (o,)
        kmax = {(3, -6): 16, (7, -24): 64}[(mant, eclamp)]
        elevels = [kmax >> (i + 1) for i in range(kmax.bit_length() - 1)]
        C2 = float(2 ** (mant + 1))
        rlevels = [int(2 * C2) >> i for i in range(int(2 * C2).bit_length())]
        rlevels = [c for c in rlevels if c >= 1]
        if scale_col is not None:
            p.op("scalar", ACT(wB, src, AF.Abs, scale=scale_col), r=rr + ("gs",), w=(rB,))
        else:
            p.op("scalar", ACT(wB, src, AF.Abs), r=rr, w=(rB,))
        p.op("vector", TS(wB, wB, float(2.0 ** eclamp), Op.max), r=(rB,), w=(rB,))
        p.op("scalar", ACT(wB, wB, AF.Ln), r=(rB,), w=(rB,))
        p.op("vector", TS(wB, wB, 1.0 / LN2, Op.mult, -float(eclamp), Op.add),
             r=(rB,), w=(rB,))
        p.op("vector", CPY(wD, wB), r=(rB,), w=(rD,))
        for c in elevels:
            CMPGE(wC, wB, c, rr=(rB,), wr=(rC,))
            p.op("vector", STT(wB, wC, -float(c), wB, Op.mult, Op.add),
                 r=(rC, rB), w=(rB,))
        p.op("vector", TT(wD, wD, wB, Op.subtract), r=(rD, rB), w=(rD,))
        p.op("scalar", ACT(wA, wD, AF.Exp, scale=LN2,
                           bias=float(eclamp - mant) * LN2), r=(rD,), w=(rA,))
        if lnbias_col is not None:
            p.op("scalar", ACT(wD, wD, AF.Exp, scale=-LN2, bias=lnbias_col),
                 r=(rD, "gs"), w=(rD,))
        else:
            p.op("scalar", ACT(wD, wD, AF.Exp, scale=-LN2,
                               bias=float(mant - eclamp) * LN2), r=(rD,), w=(rD,))
        p.op("vector", TT(wD, src, wD, Op.mult), r=rr + (rD,), w=(rD,))
        p.op("vector", TS(wB, wD, 0.5 + C2, Op.add), r=(rD,), w=(rB,))
        p.op("vector", CPY(wD, wB), r=(rB,), w=(rD,))
        for c in rlevels:
            CMPGE(wC, wB, c, rr=(rB,), wr=(rC,))
            p.op("vector", STT(wB, wC, -float(c), wB, Op.mult, Op.add),
                 r=(rC, rB), w=(rB,))
        p.op("vector", TT(wD, wD, wB, Op.subtract), r=(rD, rB), w=(rD,))
        p.op("vector", STT(R32(out) if out_r32 else out, wD, -C2, wA, Op.add, Op.mult),
             r=(rD, rA), w=o)

    def round_int_ladder(t_in, out, wB, wC, wD, cmax, rr=(), wr=()):
        o, rB, rC, rD = wr
        C2 = float(2 ** (int(cmax + 1).bit_length()))
        rlevels = [int(2 * C2) >> i for i in range(int(2 * C2).bit_length())]
        rlevels = [c for c in rlevels if c >= 1]
        p.op("vector", TS(wB, t_in, 0.5 + C2, Op.add), r=rr, w=(rB,))
        p.op("vector", CPY(wD, wB), r=(rB,), w=(rD,))
        for c in rlevels:
            CMPGE(wC, wB, c, rr=(rB,), wr=(rC,))
            p.op("vector", STT(wB, wC, -float(c), wB, Op.mult, Op.add),
                 r=(rC, rB), w=(rB,))
        p.op("vector", TT(wD, wD, wB, Op.subtract), r=(rD, rB), w=(rD,))
        p.op("vector", TS(wD, wD, -C2, Op.add), r=(rD,), w=(rD,))
        p.op("vector", TS(out, wD, float(cmax), Op.min, -float(cmax), Op.max),
             r=(rD,), w=(o,))

    def ld(dst, src, key):
        def fn(e, dst=dst, src=src):
            with nc.allow_non_contiguous_dma(reason="tiny one-time param loads"):
                return e.dma_start(out=dst, in_=src)
        p.op("sync", fn, w=("const",), dma=True, dmasem=key)

    ld(sb_id[:].bitcast(F32R), ident_h[:].bitcast(F32R), "c")
    ld(sb_on[:].bitcast(F32R), ones_h[:].bitcast(F32R), "c")
    ld(sb_onc[:].bitcast(F32R), onesc_h[:].bitcast(F32R), "c")
    ld(sb_cn[:], consts_h[:], "c")
    ld(sb_gam[:], ln_g.rearrange("(c p) -> p c", p=128), "c")
    ld(sb_bet[:], ln_b.rearrange("(c p) -> p c", p=128), "c")
    ld(sb_cb[:], conv_b.rearrange("(c p) -> p c", p=128), "c")
    ld(sb_bdt[:], b_dt.rearrange("(c p) -> p c", p=128), "c")
    ld(sb_D[:], D_par.rearrange("(c p) -> p c", p=128), "c")
    ld(sb_f0b[:], f0_b.rearrange("(c p) -> p c", p=128), "c")
    ld(sb_bqkv[:, 0:NDM], bq_h.rearrange("(c p) -> p c", p=128), "c")
    ld(sb_bqkv[:, NDM:2 * NDM], bk_h.rearrange("(c p) -> p c", p=128), "c")
    ld(sb_bqkv[:, 2 * NDM:3 * NDM], bv_h.rearrange("(c p) -> p c", p=128), "c")
    ld(sb_br[:], br_h[:, None], "c")
    ld(sb_wdtr[:].bitcast(F32R), w_dt.rearrange("d one -> one d").bitcast(F32R), "c")
    ld(sb_cw[:].rearrange("p (c k) -> p c k", k=DC), conv_w.rearrange("(c p) k -> p c k", p=128), "c")
    ld(sb_A[:].rearrange("p (c n) -> p c n", n=DS), A_log.rearrange("(c p) n -> p c n", p=128), "c")
    ld(sb_Wr[:].rearrange("p (c e) -> p c e", e=NE).bitcast(F32R),
       Wr_h.rearrange("(c p) e -> p c e", p=128).bitcast(F32R), "c")
    ld(sb_sel[:].bitcast(F32R), sel_h[:].bitcast(F32R), "c")

    p.op("scalar", ACT(sb_A[:], sb_A[:], AF.Exp), r=("const",), w=("Acoef",))
    p.op("vector", TS(sb_A[:], sb_A[:], -1.0, Op.mult), r=("Acoef",), w=("Acoef",))
    p.op("vector", MEMS(sb_z8[:], 0.0), w=("z8",))

    s0 = ExitStack()
    a_x = sb("p0_x", (128, 8 * DM), s0)
    a_ab = sb("p0_ab", (128, 2 * DM), s0)
    a_st = sb("p0_st", (128, 96), s0)
    a_mx = sb("p0_mx", (128, 300), s0)
    a_sl = sb("p0_sl", (128, 2 * 2048), s0)
    a_sg = sb("p0_sg", (128, 2 * 2048), s0)
    a_q8 = sb("p0_q8", (128, 8 * DM), s0)
    a_w0 = sb("p0_w0", (128, DM), s0)
    a_w1 = sb("p0_w1", (128, DM), s0)
    a_w2 = sb("p0_w2", (128, DM), s0)
    a_w3 = sb("p0_w3", (128, DM), s0)
    wx_sb = sb("p0_wx", (33, DI), s0)
    a_gpk = sb("p0_gpk", (1, 8), s0)

    NTILE = B * T // 128
    GRP = 8
    for g in range(NTILE // GRP):
        for k in range(GRP):
            i = g * GRP + k
            p.op("sync", lambda e, i=i, k=k: e.dma_start(
                out=a_x[:, k * DM:(k + 1) * DM], in_=x_full[i * 128:(i + 1) * 128, :]),
                w=(f"p0x{k}",), dma=True, dmasem=f"p0x{k}")
        for k in range(GRP):
            p.op("scalar", ACT(a_ab[:, 0:DM], a_x[:, k * DM:(k + 1) * DM], AF.Identity,
                               accum=a_st[:, k:k + 1]),
                 r=(f"p0x{k}",), w=("p0ab0", "p0s"))
            p.op("scalar", ACT(a_ab[:, DM:2 * DM], a_x[:, k * DM:(k + 1) * DM], AF.Square,
                               accum=a_st[:, 16 + k:17 + k]),
                 r=(f"p0x{k}",), w=("p0ab1", "p0s"), fsig=True)
        S_, Q_ = a_st[:, 0:GRP], a_st[:, 16:16 + GRP]
        MU, VA = a_st[:, 32:32 + GRP], a_st[:, 48:48 + GRP]
        RS, NB = a_st[:, 64:64 + GRP], a_st[:, 80:80 + GRP]
        p.op("vector", TS(MU, S_, 1.0 / DM, Op.mult), r=("p0s", "const"), w=("p0m",))
        p.op("scalar", ACT(VA, MU, AF.Square), r=("p0m",), w=("p0v",))
        p.op("vector", STT(VA, Q_, 1.0 / DM, VA, Op.mult, Op.subtract), r=("p0s", "p0v"), w=("p0v",))
        p.op("scalar", ACT(VA, VA, AF.Sqrt, bias=float(LN_EPS)), r=("p0v",), w=("p0v",))
        p.op("vector", REC(RS, VA), r=("p0v",), w=("p0r",))
        p.op("vector", STT(NB, MU, -1.0, RS, Op.mult, Op.mult), r=("p0m", "p0r"), w=("p0n",))
        for k in range(GRP):
            i = g * GRP + k
            ab = a_ab[:, (k % 2) * DM:(k % 2 + 1) * DM]
            p.op("scalar", ACT(ab, a_x[:, k * DM:(k + 1) * DM], AF.Abs,
                               scale=RS[:, k:k + 1], bias=NB[:, k:k + 1]),
                 r=(f"p0x{k}", "p0r", "p0n"), w=(f"p0ab{k % 2}",))
            p.op("vector", RMAX(a_mx[:, i:i + 1], ab), r=(f"p0ab{k % 2}",), w=("p0mx",))
    p.op("vector", RMAX(a_mx[:, 128:129], a_mx[:, 0:128]), r=("p0mx",), w=("p0mx",))
    p.op("tensor", TRA(psC[0:1, 0:128], a_mx[:, 128:129]), r=("p0mx", "const"), w=("pC",))
    p.op("vector", CPY(a_mx[0:1, 129:257], psC[0:1, 0:128]), r=("pC",), w=("p0mx",))
    p.op("vector", RMAX(a_mx[0:1, 0:1], a_mx[0:1, 129:257]), r=("p0mx",), w=("p0am",))
    AMX = a_mx[0:1, 0:1]
    p.op("vector", TS(AMX, AMX, 1e-12, Op.max), r=("p0am", "const"), w=("p0am",))
    p.op("vector", REC(a_mx[0:1, 1:2], AMX), r=("p0am",), w=("p0am",))
    p.op("vector", TS(a_mx[0:1, 2:3], a_mx[0:1, 1:2], 448.0, Op.mult, 448.0, Op.min),
         r=("p0am", "const"), w=("p0am",))
    SX = a_mx[0:1, 2:3]

    for r in range(NDM):
        p.op("sync", lambda e, r=r: e.dma_start(
            out=a_q8[:, r * DM:(r + 1) * DM], in_=f0_w[r * 128:(r + 1) * 128, :]),
            w=(f"q8_{r}",), dma=True, dmasem="q8l")
    for r in range(NDM):
        p.op("vector", RMAX(a_mx[:, 10 + r:11 + r], a_q8[:, r * DM:(r + 1) * DM], absv=True),
             r=(f"q8_{r}",), w=("p0wm",))
    p.op("vector", RMAX(a_mx[:, 18:19], a_mx[:, 10:18]), r=("p0wm",), w=("p0wm",))
    p.op("tensor", TRA(psC[0:1, 128:256], a_mx[:, 18:19]), r=("p0wm", "const"), w=("pC",))
    p.op("vector", CPY(a_mx[0:1, 129:257], psC[0:1, 128:256]), r=("pC",), w=("p0wm",))
    p.op("vector", RMAX(a_mx[0:1, 19:20], a_mx[0:1, 129:257]), r=("p0wm",), w=("p0wm",))
    WMX = a_mx[0:1, 19:20]
    p.op("vector", TS(WMX, WMX, 1e-12, Op.max), r=("p0wm", "const"), w=("p0wm",))
    p.op("vector", REC(a_mx[0:1, 20:21], WMX), r=("p0wm",), w=("p0wm",))
    p.op("vector", TS(a_mx[0:1, 21:22], a_mx[0:1, 20:21], 448.0, Op.mult, 448.0, Op.min),
         r=("p0wm", "const"), w=("p0wm",))
    SW = a_mx[0:1, 21:22]
    p.op("vector", REC(a_mx[0:1, 4:5], SX), r=("p0am",), w=("p0g",))
    p.op("vector", REC(a_mx[0:1, 5:6], SW), r=("p0wm",), w=("p0g",))
    p.op("vector", TT(a_mx[0:1, 6:7], a_mx[0:1, 4:5], a_mx[0:1, 5:6], Op.mult),
         r=("p0g",), w=("p0g",))
    p.op("scalar", ACT(a_mx[0:1, 7:8], SX, AF.Ln), r=("p0am",), w=("p0g",))
    p.op("vector", TS(a_mx[0:1, 7:8], a_mx[0:1, 7:8], 9.0 * LN2, Op.add), r=("p0g",), w=("p0g",))
    p.op("scalar", ACT(a_mx[0:1, 8:9], SW, AF.Ln), r=("p0wm",), w=("p0g",))
    p.op("vector", TS(a_mx[0:1, 8:9], a_mx[0:1, 8:9], 9.0 * LN2, Op.add), r=("p0g",), w=("p0g",))
    p.op("vector", CPY(R32(a_gpk[0:1, 0:1]), a_mx[0:1, 6:7]), r=("p0g",), w=("p0g",))
    p.op("vector", CPY(R32(a_gpk[0:1, 1:2]), a_mx[0:1, 7:8]), r=("p0g",), w=("p0g",))
    p.op("vector", CPY(R32(a_gpk[0:1, 2:3]), SX), r=("p0am",), w=("p0g",))
    p.op("vector", CPY(R32(a_gpk[0:1, 3:4]), a_mx[0:1, 8:9]), r=("p0g",), w=("p0g",))
    p.op("vector", CPY(R32(a_gpk[0:1, 4:5]), SW), r=("p0wm",), w=("p0g",))
    p.op("tensor", MMF32(psD[:, 0:5], sb_on[:], a_gpk[0:1, 0:5], True, True),
         r=("p0g", "const"), w=("pD",))
    p.op("vector", CPY(sb_gs[:, 0:5], psD[:, 0:5]), r=("pD",), w=("gs",))
    GS_DEQ, GS_LNSX, GS_SX, GS_LNSW, GS_SW = (sb_gs[:, i:i + 1] for i in range(5))
    if debug:
        p.op("gpsimd", lambda e: e.dma_start(out=dbg["d_sc"][:], in_=a_mx[0:1, 0:8]),
             r=("p0g", "p0am"), dma=True, dmasem="dbg")

    for r in range(NDM):
        quant_grid(a_q8[:, r * DM:(r + 1) * DM], a_q8[:, r * DM:(r + 1) * DM],
                   a_w0[:], a_w1[:], a_w2[:], a_w3[:], mant=3, eclamp=-6,
                   scale_col=GS_SW, lnbias_col=GS_LNSW,
                   rr=(f"q8_{r}",), wr=(f"q8_{r}", "w0", "w1", "w2", "w3"))

    def transp_slabs(nrow, ncol, src_of, dst, skey, extra_r=()):
        for r in range(nrow):
            res, slab = src_of(r)
            stg = a_sg[:, (r % 2) * 2048:(r % 2) * 2048 + ncol * 128]
            for c in range(ncol):
                slot = c % 4
                p.op("tensor", TRA(psC[:, slot * 128:(slot + 1) * 128],
                                   slab[:, c * 128:(c + 1) * 128]),
                     r=(res, "const") + tuple(extra_r), w=("pC",))
                eng = "scalar" if c % 2 == 0 else "vector"
                dsta = stg[:, c * 128:(c + 1) * 128]
                fn = (ACT(dsta, psC[:, slot * 128:(slot + 1) * 128], AF.Identity)
                      if eng == "scalar" else CPY(dsta, psC[:, slot * 128:(slot + 1) * 128]))
                p.op(eng, fn, r=("pC",), w=(f"sg{r % 2}",))
            p.op("gpsimd", lambda e, stg=stg, r=r, dst=dst: e.dma_start(
                out=dst[r], in_=stg), r=(f"sg{r % 2}",), w=(skey,), dma=True, dmasem=skey)

    transp_slabs(NDM, NDM, lambda r: (f"q8_{r}", a_q8[:, r * DM:(r + 1) * DM]), s_Wf0T, "SWf0")

    def qkv_prep(W, dst, skey):
        def src_of(r):
            slab = a_sl[:, (r % 2) * 2048:(r % 2) * 2048 + DM]
            sres = f"sl{r % 2}"
            p.op("sync", lambda e, r=r, slab=slab, W=W: e.dma_start(
                out=slab, in_=W[r * 128:(r + 1) * 128, :]), w=(sres,), dma=True, dmasem=sres)
            rm = a_mx[:, 30 + (r % 2):31 + (r % 2)]
            rs = a_mx[:, 32 + (r % 2):33 + (r % 2)]
            rmr, rsr = f"rm{r % 2}", f"rs{r % 2}"
            p.op("vector", RMAX(rm, slab, absv=True), r=(sres,), w=(rmr,))
            p.op("vector", TS(rm, rm, 1.0 / 127.0, Op.mult, 1e-8, Op.max),
                 r=(rmr, "const"), w=(rmr,))
            p.op("vector", REC(rs, rm), r=(rmr,), w=(rsr,))
            p.op("vector", TS(a_w0[:], slab, rs, Op.mult), r=(sres, rsr), w=("w0",))
            round_int_ladder(a_w0[:], a_w0[:], a_w1[:], a_w2[:], a_w3[:], 127.0,
                             rr=("w0",), wr=("w0", "w1", "w2", "w3"))
            p.op("vector", TS(slab, a_w0[:], rm, Op.mult), r=("w0", rmr), w=(sres,))
            return (sres, slab)
        transp_slabs(NDM, NDM, src_of, dst, skey)

    qkv_prep(Wq_h, s_WqT, "SWq")
    qkv_prep(Wk_h, s_WkT, "SWk")
    qkv_prep(Wv_h, s_WvT, "SWv")

    def plain_prep(W, nrow, ncol, dst, skey):
        def src_of(r):
            slab = a_sl[:, (r % 2) * 2048:(r % 2) * 2048 + ncol * 128]
            sres = f"sl{r % 2}"
            p.op("sync", lambda e, r=r, slab=slab, W=W: e.dma_start(
                out=slab, in_=W[r * 128:(r + 1) * 128, :]), w=(sres,), dma=True, dmasem=sres)
            return (sres, slab)
        transp_slabs(nrow, ncol, src_of, dst, skey)

    plain_prep(W_in, NF, NDM, s_WinT, "SWi")
    plain_prep(W_out, NDM, NDI, s_WoutT, "SWo")

    p.op("sync", lambda e: e.dma_start(out=wx_sb[:], in_=W_x[:]), w=("wx",), dma=True, dmasem="wx")
    for c in range(NDI):
        slot = c % 4
        p.op("tensor", TRA(psC[:, slot * 128:slot * 128 + 33], wx_sb[:, c * 128:(c + 1) * 128]),
             r=("wx", "const"), w=("pC",))
        p.op("vector", CPY(R32(sb_WxT[:, c * 33:(c + 1) * 33]), psC[:, slot * 128:slot * 128 + 33]),
             r=("pC",), w=("WxT",))

    p.barrier()
    s0.close()

    s1 = ExitStack()
    a_xt = sb("p1_xt", (128, NTT * DM), s1)
    a_xnd = sb("p1_xnd", (128, NDM * TC), s1)
    a_xn2 = sb("p1_xn2", (128, NDM * TC), s1)
    a_xb = sb("p1_xb", (128, 2 * (TC + 8)), s1)
    a_y2 = sb("p1_y2", (128, NDI * TC), s1)
    a_h = sb("p1_h", (128, DS * TC), s1)
    a_sc = sb("p1_sc", (128, 10 * TC), s1)
    a_stt = sb("p1_st", (128, 64), s1)
    a_hst = sb("p1_hst", (128, NDI * DS), s1)
    a_hal = sb("p1_hal", (128, NDI * 3), s1)
    a_r2 = sb("p1_r2", (1, 2 * TC), s1)
    a_B = sb("p1_B", (DS, TC), s1)
    a_C = sb("p1_C", (DS, TC), s1)
    a_dtr = sb("p1_dtr", (1, TC), s1)
    a_r1 = sb("p1_r1", (1, 2 * TC), s1)
    a_wst = sb("p1_wst", (128, 2 * DM), s1)
    a_wso = sb("p1_wso", (128, 2 * DM), s1)
    a_tl = sb("p1_tl", (128, 3 * TC), s1)

    SCN = {"a0": 0, "a1": 1, "b0": 2, "b1": 3, "u": 4, "dt": 5, "xc": 6, "sz": 7,
           "t0": 8, "t1": 9, "p0": 2, "p1": 3}

    def scr(nm):
        i = SCN[nm]
        return a_sc[:, i * TC:(i + 1) * TC]

    def sc(nm):
        return scr(nm).bitcast(F32R)

    p.op("vector", MEMS(a_hst[:], 0.0), w=("hst",))
    p.op("vector", MEMS(a_hal[:], 0.0), w=("halo",))

    for ch in range(NCH):
        t0 = ch * TC
        for tt in range(NTT):
            p.op("sync", lambda e, tt=tt, t0=t0: e.dma_start(
                out=a_xt[:, tt * DM:(tt + 1) * DM],
                in_=x_own[t0 + tt * 128:t0 + (tt + 1) * 128, :]),
                w=(f"xt{tt}",), dma=True, dmasem=f"xt{tt}")
        for tt in range(NTT):
            p.op("scalar", ACT(sb_jnk[:], a_xt[:, tt * DM:(tt + 1) * DM], AF.Identity,
                               accum=a_stt[:, tt:tt + 1]), r=(f"xt{tt}",), w=("jnk", "st1"), fsig=True)
            p.op("scalar", ACT(sb_jnk[:], a_xt[:, tt * DM:(tt + 1) * DM], AF.Square,
                               accum=a_stt[:, 4 + tt:5 + tt]), r=(f"xt{tt}",), w=("jnk", "st1"), fsig=True)
        MU, VA = a_stt[:, 8:12], a_stt[:, 12:16]
        RS, NB = a_stt[:, 16:20], a_stt[:, 20:24]
        p.op("vector", TS(MU, a_stt[:, 0:4], 1.0 / DM, Op.mult), r=("st1", "const"), w=("st1m",))
        p.op("scalar", ACT(VA, MU, AF.Square), r=("st1m",), w=("st1v",))
        p.op("vector", STT(VA, a_stt[:, 4:8], 1.0 / DM, VA, Op.mult, Op.subtract),
             r=("st1", "st1v"), w=("st1v",))
        p.op("scalar", ACT(VA, VA, AF.Sqrt, bias=float(LN_EPS)), r=("st1v",), w=("st1v",))
        p.op("vector", REC(RS, VA), r=("st1v",), w=("st1r",))
        p.op("vector", STT(NB, MU, -1.0, RS, Op.mult, Op.mult), r=("st1m", "st1r"), w=("st1n",))
        for tt in range(NTT):
            p.op("scalar", ACT(R32(a_xn2[:, tt * DM:(tt + 1) * DM]), a_xt[:, tt * DM:(tt + 1) * DM],
                               AF.Identity, scale=RS[:, tt:tt + 1], bias=NB[:, tt:tt + 1]),
                 r=(f"xt{tt}", "st1r", "st1n"), w=(f"xnt{tt}", "xn2"))
            p.op("scalar", ACT(sb_jnk[:], a_xn2[:, tt * DM:(tt + 1) * DM], AF.Identity,
                               accum=a_stt[:, 24 + tt:25 + tt]), r=(f"xnt{tt}",), w=("jnk", "st2"), fsig=True)
            p.op("scalar", ACT(sb_jnk[:], a_xn2[:, tt * DM:(tt + 1) * DM], AF.Square,
                               accum=a_stt[:, 28 + tt:29 + tt]), r=(f"xnt{tt}",), w=("jnk", "st2"), fsig=True)
        MU2, VA2 = a_stt[:, 32:36], a_stt[:, 36:40]
        p.op("vector", TS(MU2, a_stt[:, 24:28], 1.0 / DM, Op.mult), r=("st2", "const"), w=("st2m",))
        p.op("scalar", ACT(VA2, MU2, AF.Square), r=("st2m",), w=("st2v",))
        p.op("vector", STT(VA2, a_stt[:, 28:32], 1.0 / DM, VA2, Op.mult, Op.subtract),
             r=("st2", "st2v"), w=("st2v",))
        p.op("scalar", ACT(VA2, VA2, AF.Sqrt, bias=float(LN_EPS)), r=("st2v",), w=("st2v",))
        RS2 = a_stt[:, 24:28]
        p.op("vector", REC(RS2, VA2), r=("st2v",), w=("st2r",))
        M2R = a_stt[:, 28:32]
        p.op("vector", TT(M2R, MU2, RS2, Op.mult), r=("st2m", "st2r"), w=("st2n",))
        for tt in range(NTT):
            p.op("tensor", TRA(psC[0:1, 0:128], a_stt[:, 24 + tt:25 + tt]),
                 r=("st2r", "const"), w=("pC",))
            p.op("vector", CPY(R32(a_r2[0:1, tt * 128:(tt + 1) * 128]), psC[0:1, 0:128]),
                 r=("pC",), w=("r2",))
            p.op("tensor", TRA(psC[0:1, 128:256], a_stt[:, 28 + tt:29 + tt]),
                 r=("st2n", "const"), w=("pC",))
            p.op("vector", CPY(R32(a_r2[0:1, TC + tt * 128:TC + (tt + 1) * 128]), psC[0:1, 128:256]),
                 r=("pC",), w=("r2",))
        p.op("tensor", MM(psD[:, 0:TC], sb_on[:], a_r2[0:1, 0:TC], True, True),
             r=("r2", "const"), w=("pD",))
        p.op("tensor", MM(psE[:, 0:TC], sb_on[:], a_r2[0:1, TC:2 * TC], True, True),
             r=("r2", "const"), w=("pE",))

        for tt in range(NTT):
            for dm in range(NDM):
                slot = (tt * NDM + dm) % 4
                p.op("tensor", TRA(psC[:, slot * 128:(slot + 1) * 128],
                                   a_xn2[:, tt * DM + dm * 128:tt * DM + (dm + 1) * 128]),
                     r=(f"xnt{tt}", "const"), w=("pC",))
                dsta = R32(a_xnd[:, dm * TC + tt * 128:dm * TC + (tt + 1) * 128])
                eng = "scalar" if (tt * NDM + dm) % 2 == 0 else "vector"
                fn = (ACT(dsta, psC[:, slot * 128:(slot + 1) * 128], AF.Identity)
                      if eng == "scalar" else CPY(dsta, psC[:, slot * 128:(slot + 1) * 128]))
                p.op(eng, fn, r=("pC",), w=("xnd",))
        if debug:
            for dm in range(NDM):
                p.op("gpsimd", lambda e, dm=dm, t0=t0: e.dma_start(
                    out=dbg["d_xn"][dm, :, t0:t0 + TC], in_=a_xnd[:, dm * TC:(dm + 1) * TC]),
                    r=("xnd",), dma=True, dmasem="dbg")

        for dm in range(NDM):
            p.op("vector", STT(sc("t0"), a_xnd[:, dm * TC:(dm + 1) * TC], 0.0,
                               psD[:, 0:TC], Op.bypass, Op.mult), r=("xnd", "pD"), w=("sct0",))
            p.op("vector", STT(sc("t0"), sc("t0"), 0.0, psE[:, 0:TC], Op.bypass, Op.subtract),
                 r=("sct0", "pE"), w=("sct0",))
            p.op("scalar", ACT(R32(a_xn2[:, dm * TC:(dm + 1) * TC]), sc("t0"), AF.Identity,
                               scale=sb_gam[:, dm:dm + 1], bias=sb_bet[:, dm:dm + 1]),
                 r=("sct0", "const"), w=("xn2", f"xnt{dm // 2}"))
        if debug:
            for dm in range(NDM):
                p.op("gpsimd", lambda e, dm=dm, t0=t0: e.dma_start(
                    out=dbg["d_xn2"][dm, :, t0:t0 + TC], in_=a_xn2[:, dm * TC:(dm + 1) * TC]),
                    r=("xn2",), dma=True, dmasem="dbg")

        for f in range(NF):
            fp = f % 2
            blk = a_wst[:, fp * DM:(fp + 1) * DM]
            p.op("sync", lambda e, f=f, blk=blk: e.dma_start(
                out=blk.bitcast(F32R), in_=s_WinT[f].bitcast(F32R)),
                 r=("SWi",), w=(f"wst{fp}",), dma=True, dmasem=f"wst{fp}")
            pa = psA[:, fp * 512:fp * 512 + TC]
            for dm in range(NDM):
                p.op("tensor", MM(pa, blk[:, dm * 128:(dm + 1) * 128],
                                  a_xn2[:, dm * TC:(dm + 1) * TC], dm == 0, dm == NDM - 1),
                     r=(f"wst{fp}", "xn2"), w=(f"pA{fp}",))
            if f < NDI:
                di = f
                xbb = a_xb[:, fp * (TC + 8):fp * (TC + 8) + TC + 8]
                p.op("scalar", ACT(xbb[:, 3:3 + TC], pa, AF.Identity),
                     r=(f"pA{fp}",), w=(f"xb{fp}",))
                p.op("vector", CPY(xbb[:, 0:3], a_hal[:, di * 3:(di + 1) * 3]),
                     r=("halo",), w=(f"xb{fp}",))
                p.op("vector", TS(sc("t1"), xbb[:, 0:TC], sb_cw[:, di * DC:di * DC + 1], Op.mult),
                     r=(f"xb{fp}", "const"), w=("sct1",))
                for k in range(1, DC):
                    p.op("vector", STT(sc("t1"), xbb[:, k:k + TC],
                                       sb_cw[:, di * DC + k:di * DC + k + 1],
                                       sc("t1"), Op.mult, Op.add),
                         r=(f"xb{fp}", "sct1", "const"), w=("sct1",))
                p.op("vector", CPY(a_hal[:, di * 3:(di + 1) * 3], xbb[:, TC:TC + 3]),
                     r=(f"xb{fp}",), w=("halo",))
                p.op("scalar", ACT(R32(sc("xc")), sc("t1"), AF.Silu, bias=sb_cb[:, di:di + 1]),
                     r=("sct1", "const"), w=("scxc",))
                p.op("gpsimd", lambda e, di=di: e.dma_start(out=s_xc[di], in_=scr("xc")),
                     r=("scxc",), w=(f"s_xc{di}",), dma=True, dmasem="Sxc")
                p.op("tensor", MM(psF[0:1, 0:TC], sb_WxT[:, di * 33:di * 33 + 1],
                                  sc("xc"), di == 0, di == NDI - 1),
                     r=("scxc", "WxT"), w=("pF",))
                p.op("tensor", MM(psD[0:DS, 0:TC], sb_WxT[:, di * 33 + 1:di * 33 + 1 + DS],
                                  sc("xc"), di == 0, di == NDI - 1),
                     r=("scxc", "WxT"), w=("pD",))
                p.op("tensor", MM(psE[0:DS, 0:TC], sb_WxT[:, di * 33 + 1 + DS:di * 33 + 33],
                                  sc("xc"), di == 0, di == NDI - 1),
                     r=("scxc", "WxT"), w=("pE",))
            else:
                di = f - NDI
                p.op("scalar", ACT(sc("sz"), pa, AF.Silu), r=(f"pA{fp}",), w=("scsz",))
                p.op("gpsimd", lambda e, di=di: e.dma_start(out=s_sz[di], in_=scr("sz")),
                     r=("scsz",), w=(f"s_sz{di}",), dma=True, dmasem="Ssz")

        p.op("scalar", ACT(R32(a_dtr[:]), psF[0:1, 0:TC], AF.Identity), r=("pF",), w=("dtr",))
        p.op("scalar", ACT(R32(a_B[:]), psD[0:DS, 0:TC], AF.Identity), r=("pD",), w=("Brow",))
        p.op("scalar", ACT(R32(a_C[:]), psE[0:DS, 0:TC], AF.Identity), r=("pE",), w=("Crow",))
        for di in range(NDI):
            p.op("tensor", MM(psC[:, 0:TC], sb_wdtr[0:1, di * 128:(di + 1) * 128],
                              a_dtr[0:1, 0:TC], True, True),
                 r=("dtr", "const"), w=("pC", "pC", "pC", "pC"))
            p.op("scalar", ACT(sc("t0"), psC[:, 0:TC], AF.Abs, bias=sb_bdt[:, di:di + 1]),
                 r=("pC", "pC", "pC", "pC", "const"), w=("sct0",))
            p.op("scalar", ACT(sc("t0"), sc("t0"), AF.Exp, scale=-1.0), r=("sct0",), w=("sct0",))
            p.op("scalar", ACT(sc("t0"), sc("t0"), AF.Ln, bias=1.0), r=("sct0",), w=("sct0",))
            p.op("scalar", ACT(sc("t1"), psC[:, 0:TC], AF.Relu, bias=sb_bdt[:, di:di + 1]),
                 r=("pC", "pC", "pC", "pC", "const"), w=("sct1",))
            p.op("vector", TT(sc("dt"), sc("t1"), sc("t0"), Op.add), r=("sct0", "sct1"), w=("scdt",))
            p.op("gpsimd", lambda e, di=di: e.dma_start(out=s_dt[di], in_=scr("dt")),
                 r=("scdt",), w=(f"s_dt{di}",), dma=True, dmasem="Sdt")

        for di in range(NDI):
            p.op("gpsimd", lambda e, di=di: e.dma_start(out=sc("dt"), in_=s_dt[di].bitcast(F32R)),
                 r=(f"s_dt{di}",), w=("scdt",), dma=True, dmasem="Ldt")
            p.op("gpsimd", lambda e, di=di: e.dma_start(out=sc("xc"), in_=s_xc[di].bitcast(F32R)),
                 r=(f"s_xc{di}",), w=("scxc",), dma=True, dmasem="Lxc")
            p.op("vector", TT(sc("u"), sc("dt"), sc("xc"), Op.mult), r=("scdt", "scxc"), w=("scu",))
            for n in range(DS):
                par = n % 2
                ab, bb = sc(f"a{par}"), sc(f"b{par}")
                p.op("tensor", MM(psD[:, 0:TC], sb_sel[:, n * 128:(n + 1) * 128],
                                  a_B[:, 0:TC], True, True),
                     r=("Brow", "const"), w=("pD",))
                p.op("tensor", MM(psE[:, 0:TC], sb_sel[:, n * 128:(n + 1) * 128],
                                  a_C[:, 0:TC], True, True),
                     r=("Crow", "const"), w=("pE",))
                p.op("scalar", ACT(ab, sc("dt"), AF.Exp,
                                   scale=sb_A[:, di * DS + n:di * DS + n + 1]),
                     r=("scdt", "Acoef"), w=(f"sca{par}",))
                p.op("vector", STT(bb, sc("u"), 0.0, psD[:, 0:TC], Op.bypass, Op.mult),
                     r=("scu", "pD"), w=(f"scb{par}",))
                hsl = a_h[:, n * TC:(n + 1) * TC]
                p.op("vector", SCAN(hsl, ab, bb, a_hst[:, di * DS + n:di * DS + n + 1]),
                     r=(f"sca{par}", f"scb{par}", "hst"), w=(f"h{n}",))
                pp = sc(f"p{par}")
                p.op("vector", STT(R32(pp), hsl, 0.0, psE[:, 0:TC], Op.bypass, Op.mult),
                     r=(f"h{n}", "pE"), w=(f"scb{par}",))
                p.op("tensor", MM(psF[:, 0:TC], sb_id[:], pp, n == 0, n == DS - 1),
                     r=(f"scb{par}", "const"), w=("pF",))
            p.op("vector", CPY(a_hst[:, di * DS:(di + 1) * DS],
                               a_h[:].rearrange("q (n t) -> q n t", n=DS)[:, :, TC - 1]),
                 r=tuple(f"h{n}" for n in range(DS)), w=("hst",))
            p.op("gpsimd", lambda e, di=di: e.dma_start(out=sc("sz"), in_=s_sz[di].bitcast(F32R)),
                 r=(f"s_sz{di}",), w=("scsz",), dma=True, dmasem="Lsz")
            p.op("vector", STT(sc("t1"), sc("xc"), sb_D[:, di:di + 1], psF[:, 0:TC],
                               Op.mult, Op.add), r=("scxc", "pF", "const"), w=("sct1",))
            p.op("vector", TT(R32(a_y2[:, di * TC:(di + 1) * TC]), sc("t1"), sc("sz"), Op.mult),
                 r=("sct1", "scsz"), w=("y2",))
            if debug:
                p.op("gpsimd", lambda e, di=di, t0=t0: e.dma_start(
                    out=dbg["d_y2"][di, :, t0:t0 + TC], in_=a_y2[:, di * TC:(di + 1) * TC]),
                    r=("y2",), dma=True, dmasem="dbg")
                p.op("gpsimd", lambda e, di=di, t0=t0: e.dma_start(
                    out=dbg["d_dt"][di, :, t0:t0 + TC], in_=scr("dt")),
                    r=("scdt",), dma=True, dmasem="dbg")
                p.op("gpsimd", lambda e, di=di, t0=t0: e.dma_start(
                    out=dbg["d_xc"][di, :, t0:t0 + TC], in_=scr("xc")),
                    r=("scxc",), dma=True, dmasem="dbg")
                p.op("gpsimd", lambda e, di=di, t0=t0: e.dma_start(
                    out=dbg["d_sz"][di, :, t0:t0 + TC], in_=scr("sz")),
                    r=("scsz",), dma=True, dmasem="dbg")

        for dm in range(NDM):
            quant_grid(a_xnd[:, dm * TC:(dm + 1) * TC], a_xn2[:, dm * TC:(dm + 1) * TC],
                       sc("t0"), sc("t1"), sc("p0"), sc("p1"), mant=3, eclamp=-6,
                       scale_col=GS_SX, lnbias_col=GS_LNSX, out_r32=True,
                       rr=("xnd",), wr=(("xn2", f"xnt{dm // 2}"), "sct0", "sct1", "scb0", "scb1"))
        if debug:
            for dm in range(NDM):
                p.op("gpsimd", lambda e, dm=dm, t0=t0: e.dma_start(
                    out=dbg["d_xq"][dm, :, t0:t0 + TC], in_=a_xn2[:, dm * TC:(dm + 1) * TC]),
                    r=("xn2",), dma=True, dmasem="dbg")

        for dm in range(NDM):
            fp = dm % 2
            blkq = a_wst[:, fp * DM:(fp + 1) * DM]
            p.op("sync", lambda e, dm=dm, blkq=blkq: e.dma_start(
                out=blkq.bitcast(F32R), in_=s_Wf0T[dm].bitcast(F32R)),
                 r=("SWf0",), w=(f"wst{fp}",), dma=True, dmasem=f"wst{fp}")
            for d2 in range(NDM):
                p.op("tensor", MM(psB[:, 0:TC], blkq[:, d2 * 128:(d2 + 1) * 128],
                                  a_xn2[:, d2 * TC:(d2 + 1) * TC], d2 == 0, d2 == NDM - 1),
                     r=(f"wst{fp}", "xn2"), w=("pB0",))
            p.op("scalar", ACT(R32(a_tl[:, 0:TC]), psB[:, 0:TC], AF.Identity,
                               scale=GS_DEQ, bias=sb_f0b[:, dm:dm + 1]),
                 r=("pB0", "gs", "const"), w=("tl0",))
            quant_grid(a_tl[:, 0:TC], a_tl[:, 0:TC], sc("t0"), sc("t1"), sc("p0"), sc("p1"),
                       mant=7, eclamp=-24, out_r32=True,
                       rr=("tl0",), wr=("tl0", "sct0", "sct1", "scb0", "scb1"))
            if debug:
                p.op("gpsimd", lambda e, dm=dm, t0=t0: e.dma_start(
                    out=dbg["d_f0"][dm, :, t0:t0 + TC], in_=a_tl[:, 0:TC]),
                    r=("tl0",), dma=True, dmasem="dbg")
            pb2 = psB[:, 512:512 + TC]
            for hf in range(2):
                blko = a_wso[:, hf * DM:(hf + 1) * DM]
                p.op("sync", lambda e, dm=dm, hf=hf, blko=blko: e.dma_start(
                    out=blko.bitcast(F32R), in_=s_WoutT[dm][:, hf * DM:(hf + 1) * DM].bitcast(F32R)),
                    r=("SWo",), w=(f"wso{hf}",), dma=True, dmasem=f"wso{hf}")
                for dj in range(NDM):
                    di = hf * NDM + dj
                    p.op("tensor", MM(pb2, blko[:, dj * 128:(dj + 1) * 128],
                                      a_y2[:, di * TC:(di + 1) * TC], di == 0, False),
                         r=(f"wso{hf}", "y2"), w=("pB1",))
            p.op("tensor", MM(pb2, sb_id[:], a_xnd[:, dm * TC:(dm + 1) * TC], False, False),
                 r=("xnd", "const"), w=("pB1",))
            p.op("tensor", MM(pb2, sb_id[:], a_tl[:, 0:TC], False, False),
                 r=("tl0", "const"), w=("pB1",))
            for tt in range(NTT):
                p.op("tensor", TRA(pb2.rearrange("q (n f) -> q n f", n=NTT)[:, tt],
                                   a_xt[:, tt * DM + dm * 128:tt * DM + (dm + 1) * 128],
                                   False, tt == NTT - 1),
                     r=(f"xt{tt}", "const"), w=("pB1",))
            p.op("scalar", ACT(R32(a_tl[:, TC:2 * TC]), pb2, AF.Identity), r=("pB1",), w=("tl1",))
            p.op("gpsimd", lambda e, dm=dm, t0=t0: e.dma_start(
                out=s_x1[dm, :, t0:t0 + TC], in_=a_tl[:, TC:2 * TC]),
                r=("tl1",), w=(f"s_x1{dm}",), dma=True, dmasem="Sx1")
            p.op("scalar", ACT(R32(a_tl[:, 2 * TC:3 * TC]), pb2, AF.Square), r=("pB1",), w=("tl2",))
            p.op("tensor", MM(psF[0:1, 0:TC], sb_onc[:], a_tl[:, TC:2 * TC], dm == 0, dm == NDM - 1),
                 r=("tl1", "const"), w=("pF",))
            p.op("tensor", MM(psD[0:1, 0:TC], sb_onc[:], a_tl[:, 2 * TC:3 * TC], dm == 0, dm == NDM - 1),
                 r=("tl2", "const"), w=("pD",))

        R = a_r1
        p.op("vector", TS(R32(R[0:1, 0:TC]), psF[0:1, 0:TC], 1.0 / DM, Op.mult),
             r=("pF", "const"), w=("r1",))
        p.op("scalar", ACT(R32(R[0:1, TC:2 * TC]), R[0:1, 0:TC], AF.Square), r=("r1",), w=("r1b",))
        p.op("vector", STT(R32(R[0:1, TC:2 * TC]), psD[0:1, 0:TC], 1.0 / DM, R[0:1, TC:2 * TC],
                           Op.mult, Op.subtract), r=("pD", "r1b"), w=("r1b",))
        p.op("scalar", ACT(R32(R[0:1, TC:2 * TC]), R[0:1, TC:2 * TC], AF.Sqrt, bias=float(LN_EPS)),
             r=("r1b",), w=("r1b",))
        p.op("vector", REC(R32(R[0:1, TC:2 * TC]), R[0:1, TC:2 * TC]), r=("r1b",), w=("r1b",))
        p.op("vector", TT(R32(R[0:1, 0:TC]), R[0:1, 0:TC], R[0:1, TC:2 * TC], Op.mult),
             r=("r1", "r1b"), w=("r1",))
        p.op("tensor", MM(psD[:, 0:TC], sb_on[:], R[0:1, TC:2 * TC], True, True),
             r=("r1b", "const"), w=("pD",))
        p.op("tensor", MM(psE[:, 0:TC], sb_on[:], R[0:1, 0:TC], True, True),
             r=("r1", "const"), w=("pE",))
        for dm in range(NDM):
            p.op("gpsimd", lambda e, dm=dm, t0=t0: e.dma_start(
                out=a_tl[:, TC:2 * TC], in_=s_x1[dm, :, t0:t0 + TC]),
                r=(f"s_x1{dm}",), w=("tl1",), dma=True, dmasem="Lx1")
            p.op("vector", STT(R32(a_tl[:, 0:TC]), a_tl[:, TC:2 * TC], 0.0, psD[:, 0:TC],
                               Op.bypass, Op.mult), r=("tl1", "pD"), w=("tl0",))
            p.op("vector", STT(R32(a_tl[:, 0:TC]), a_tl[:, 0:TC], 0.0, psE[:, 0:TC],
                               Op.bypass, Op.subtract), r=("tl0", "pE"), w=("tl0",))
            p.op("gpsimd", lambda e, dm=dm, t0=t0: e.dma_start(
                out=s_xn1[dm, :, t0:t0 + TC], in_=a_tl[:, 0:TC]),
                r=("tl0",), w=(f"s_xn1{dm}",), dma=True, dmasem="Sxn1")
            p.op("tensor", MM(psC[0:NE, 0:TC], sb_Wr[:, dm * NE:(dm + 1) * NE],
                              a_tl[:, 0:TC], dm == 0, dm == NDM - 1),
                 r=("tl0", "const"), w=("pC", "pC", "pC", "pC"))
        p.op("scalar", ACT(R32(a_B[0:NE, 0:TC]), psC[0:NE, 0:TC], AF.Sigmoid, bias=sb_br[:]),
             r=("pC", "pC", "pC", "pC", "const"), w=("Brow",))
        if debug:
            p.op("gpsimd", lambda e, t0=t0: e.dma_start(
                out=dbg["d_rt"][:, t0:t0 + TC], in_=a_B[0:NE, 0:TC]),
                r=("Brow",), dma=True, dmasem="dbg")
        for tt in range(NTT):
            p.op("tensor", TRA(psC[:, 384:392], a_B[0:NE, tt * 128:(tt + 1) * 128]),
                 r=("Brow", "const"), w=("pC",))
            p.op("vector", CPY(a_stt[:, 40:40 + NE], psC[:, 384:384 + NE]), r=("pC",), w=("mk0",))
            mpc = sb_mpt[:, t0 // 128 + tt:t0 // 128 + tt + 1]
            p.op("vector", RMAX(mpc, a_stt[:, 40:40 + NE]), r=("mk0",), w=("mpt",))
            p.op("vector", TS(a_stt[:, 48:48 + NE], a_stt[:, 40:40 + NE], mpc, Op.is_ge),
                 r=("mk0", "mpt"), w=("mk1",))
            p.op("vector", CUMSUM(a_stt[:, 56:56 + NE], a_stt[:, 48:48 + NE], sb_z8[:, 0:NE], 0.0),
                 r=("mk1", "z8"), w=("mk2",))
            p.op("vector", TS(a_stt[:, 56:56 + NE], a_stt[:, 56:56 + NE], 1.0, Op.is_equal),
                 r=("mk2",), w=("mk2",))
            p.op("vector", TT(a_stt[:, 48:48 + NE], a_stt[:, 48:48 + NE], a_stt[:, 56:56 + NE],
                              Op.mult), r=("mk1", "mk2"), w=("mk1",))
            p.op("tensor", TRA(psC[0:NE, 0:128], a_stt[:, 48:48 + NE]), r=("mk1", "const"), w=("pC",))
            p.op("vector", CPY(R32(sb_msk[:, t0 + tt * 128:t0 + (tt + 1) * 128]), psC[0:NE, 0:128]),
                 r=("pC",), w=("msk",))

    if debug:
        p.op("gpsimd", lambda e: e.dma_start(out=dbg["d_msk"][:], in_=sb_msk[:]),
             r=("msk",), dma=True, dmasem="dbg")
        p.op("gpsimd", lambda e: e.dma_start(out=dbg["d_mp"][:], in_=sb_mpt[:]),
             r=("mpt",), dma=True, dmasem="dbg")

    p.barrier()
    s1.close()

    s3 = ExitStack()
    a_xn1 = sb("p3_xn1", (128, NDM * T), s3)
    a_q = sb("p3_q", (128, T), s3)
    a_k = sb("p3_k", (128, T), s3)
    a_v = sb("p3_v", (128, T), s3)
    a_vt = sb("p3_vt", (128, 2 * (T // 128) * (HD + 1)), s3)
    a_pt = sb("p3_pt", (128, 2 * TC), s3)
    a_cx = sb("p3_cx", (HD + 1, TC), s3)
    a_ct = sb("p3_ct", (128, HD), s3)
    a_wq3 = sb("p3_wq", (128, 3 * DM), s3)
    a_sm3 = sb("p3_sm", (128, 16), s3)
    a_nm = sb("p3_nm", (1, 1), s3)
    a_wcs = sb("p3_wcs", (128, 2 * DM), s3)
    a_wcb = sb("p3_wcb", (128, 2 * DM), s3, BF16)

    wc_jobs = []
    for e_ in range(NE):
        for c in range(NDM):
            for s_ in range(4):
                wc_jobs.append(("w1", e_, c, s_))
        for r_ in range(32):
            wc_jobs.append(("w2", e_, r_))
    _wc_i = [0]
    _wc_pend = []

    def _wc_aps(k):
        jb = wc_jobs[k]
        sl = k % 2
        stg = a_wcs[:, sl * DM:(sl + 1) * DM]
        stb = a_wcb[:, sl * DM:(sl + 1) * DM]
        if jb[0] == "w1":
            _, e_, c, s_ = jb
            src = W1_h[e_, c * 128:(c + 1) * 128, s_ * DM:(s_ + 1) * DM]
            dst = s_W1b[e_].rearrange("q (t jj) -> q t jj", jj=DM)[
                :, s_ * 8:(s_ + 1) * 8, c * 128:(c + 1) * 128]
            dres = "W1bD"
        else:
            _, e_, r_ = jb
            qt, c = r_ // 8, r_ % 8
            src = W2_h[e_, r_ * 128:(r_ + 1) * 128, :]
            dst = s_W2b[e_].rearrange("q (t jj) -> q t jj", jj=4 * DM)[
                :, :, qt * DM + c * 128:qt * DM + (c + 1) * 128]
            dres = "W2bD"
        return sl, stg, stb, src, dst, dres

    def _wc_cs(k):
        sl, stg, stb, src, dst, dres = _wc_aps(k)
        p.op("vector", CPY(stb, stg), r=(f"wcs{sl}",), w=(f"wcb{sl}",))
        p.op("scalar", lambda e, dst=dst, stb=stb: e.dma_start(
            out=dst, in_=stb.rearrange("q (t j) -> q t j", j=128)),
            r=(f"wcb{sl}",), w=(dres,), dma=True, dmasem="WcS")

    def emit_wc(n):
        for _ in range(n):
            if _wc_i[0] >= len(wc_jobs):
                break
            k = _wc_i[0]
            _wc_i[0] += 1
            sl, stg, stb, src, dst, dres = _wc_aps(k)
            p.op("scalar", lambda e, stg=stg, src=src: e.dma_start(out=stg, in_=src),
                 w=(f"wcs{sl}",), dma=True, dmasem="WcL")
            _wc_pend.append(k)
            if len(_wc_pend) == 2:
                _wc_cs(_wc_pend.pop(0))

    def emit_wc_flush():
        while _wc_pend:
            _wc_cs(_wc_pend.pop(0))

    for dm in range(NDM):
        p.op("sync", lambda e, dm=dm: e.dma_start(
            out=a_xn1[:, dm * T:(dm + 1) * T].bitcast(F32R), in_=s_xn1[dm].bitcast(F32R)),
            r=(f"s_xn1{dm}",), w=(f"xn1r{dm}",), dma=True, dmasem="Lxn1")

    for o in range(NDM):
        for wi, (skey, sW) in enumerate((("SWq", s_WqT), ("SWk", s_WkT), ("SWv", s_WvT))):
            p.op("sync", lambda e, o=o, wi=wi, sW=sW: e.dma_start(
                out=a_wq3[:, wi * DM:(wi + 1) * DM].bitcast(F32R), in_=sW[o].bitcast(F32R)),
                r=(skey,), w=(f"wq3_{wi}",), dma=True, dmasem="Lwq")
        for wi, (dstb, dres, bcol) in enumerate(
                ((a_q, "q3", sb_bqkv[:, o:o + 1]),
                 (a_k, "k3", sb_bqkv[:, NDM + o:NDM + o + 1]),
                 (a_v, "v3", sb_bqkv[:, 2 * NDM + o:2 * NDM + o + 1]))):
            for tc4 in range(T // TC):
                par = tc4 % 2
                pa = psA[:, par * 512:par * 512 + TC]
                for dm in range(NDM):
                    p.op("tensor", MM(pa, a_wq3[:, wi * DM + dm * 128:wi * DM + (dm + 1) * 128],
                                      a_xn1[:, dm * T + tc4 * TC:dm * T + (tc4 + 1) * TC],
                                      dm == 0, dm == NDM - 1),
                         r=(f"wq3_{wi}",) + tuple(f"xn1r{d}" for d in range(NDM)),
                         w=(f"pA{par}",))
                p.op("scalar", ACT(R32(dstb[:, tc4 * TC:(tc4 + 1) * TC]), pa, AF.Identity, bias=bcol),
                     r=(f"pA{par}", "const"), w=(dres,))
        if debug:
            p.op("gpsimd", lambda e, o=o: e.dma_start(out=dbg["d_q"][o], in_=a_q[:]),
                 r=("q3",), dma=True, dmasem="dbg")
            p.op("gpsimd", lambda e, o=o: e.dma_start(out=dbg["d_k"][o], in_=a_k[:]),
                 r=("k3",), dma=True, dmasem="dbg")
            p.op("gpsimd", lambda e, o=o: e.dma_start(out=dbg["d_v"][o], in_=a_v[:]),
                 r=("v3",), dma=True, dmasem="dbg")
        for wi, (srcb, sres, col) in enumerate(((a_q, "q3", 0), (a_k, "k3", 1))):
            p.op("scalar", ACT(R32(a_vt[:, 0:T]), srcb[:], AF.Square), r=(sres,), w=("vt",))
            p.op("vector", MEMS(a_sm3[:, col:col + 1], 0.0), w=(f"bnd{col}",))
            for tc4 in range(T // TC):
                p.op("tensor", MM(psD[0:1, 0:TC], sb_onc[:], a_vt[:, tc4 * TC:(tc4 + 1) * TC],
                                  True, True), r=("vt", "const"), w=("pD",))
                p.op("vector", RMAX(a_sm3[0:1, 4 + col:5 + col], psD[0:1, 0:TC]),
                     r=("pD",), w=(f"bnd{col}",))
                p.op("vector", TT(a_sm3[0:1, col:col + 1], a_sm3[0:1, col:col + 1],
                                  a_sm3[0:1, 4 + col:5 + col], Op.max),
                     r=(f"bnd{col}",), w=(f"bnd{col}",))
        p.op("scalar", ACT(a_sm3[0:1, 2:3], a_sm3[0:1, 0:1], AF.Sqrt), r=("bnd0",), w=("bndm",))
        p.op("scalar", ACT(a_sm3[0:1, 3:4], a_sm3[0:1, 1:2], AF.Sqrt), r=("bnd1",), w=("bndm",))
        p.op("vector", TT(a_sm3[0:1, 2:3], a_sm3[0:1, 2:3], a_sm3[0:1, 3:4], Op.mult),
             r=("bndm",), w=("bndm",))
        p.op("vector", TS(R32(a_nm[0:1, 0:1]), a_sm3[0:1, 2:3], 0.125, Op.mult, -1.0, Op.mult),
             r=("bndm", "const"), w=("bndm",))
        p.op("tensor", MMF32(psD[:, 0:1], sb_on[:], a_nm[0:1, 0:1], True, True),
             r=("bndm", "const"), w=("pD",))
        p.op("vector", CPY(a_sm3[:, 6:7], psD[:, 0:1]), r=("pD",), w=("negm",))
        NEGM = a_sm3[:, 6:7]
        p.op("vector", MEMS(a_vt[:], 1.0), r=("v3",), w=("vt",))
        for hh in range(2):
            for tk in range(T // 128):
                slot = tk % 4
                p.op("tensor", TRA(psC[:, slot * 128:slot * 128 + HD],
                                   a_v[hh * HD:(hh + 1) * HD, tk * 128:(tk + 1) * 128]),
                     r=("v3", "const"), w=("pC",))
                base = (hh * (T // 128) + tk) * (HD + 1)
                p.op("vector", CPY(R32(a_vt[:, base:base + HD]), psC[:, slot * 128:slot * 128 + HD]),
                     r=("pC",), w=("vt",))
        for hh in range(2):
            h = 2 * o + hh
            for qc in range(T // TC):
                for tk in range(T // 128):
                    par = tk % 2
                    ps_s = psB[:, par * 512:par * 512 + TC]
                    p.op("tensor", MM(ps_s, a_k[hh * HD:(hh + 1) * HD, tk * 128:(tk + 1) * 128],
                                      a_q[hh * HD:(hh + 1) * HD, qc * TC:(qc + 1) * TC],
                                      True, True), r=("k3", "q3"), w=(f"pB{par}",))
                    p.op("scalar", ACT(R32(a_pt[:, par * TC:(par + 1) * TC]), ps_s, AF.Exp,
                                       scale=C(0.125), bias=NEGM),
                         r=(f"pB{par}", "negm", "const"), w=(f"pt{par}",))
                    base = (hh * (T // 128) + tk) * (HD + 1)
                    p.op("tensor", MM(psF[0:HD + 1, 0:TC], a_vt[:, base:base + HD + 1],
                                      a_pt[:, par * TC:(par + 1) * TC],
                                      tk == 0, tk == T // 128 - 1),
                         r=(f"pt{par}", "vt"), w=("pF",))
                p.op("scalar", ACT(a_cx[:], psF[0:HD + 1, 0:TC], AF.Identity),
                     r=("pF",), w=("cx",))
                for tt in range(TC // 128):
                    p.op("tensor", TRA(psC[:, 0:HD + 1], a_cx[:, tt * 128:(tt + 1) * 128]),
                         r=("cx", "const"), w=("pC",))
                    p.op("vector", REC(a_sm3[:, 7:8], psC[:, HD:HD + 1]), r=("pC",), w=("rden",))
                    p.op("vector", TS(a_ct[:], psC[:, 0:HD], a_sm3[:, 7:8], Op.mult),
                         r=("pC", "rden"), w=("ct",))
                    ti = qc * (TC // 128) + tt
                    p.op("gpsimd", lambda e, ti=ti, h=h: e.dma_start(
                        out=s_ctxt[ti, :, h * HD:(h + 1) * HD], in_=a_ct[:]),
                        r=("ct",), w=(f"s_ct{ti}",), dma=True, dmasem="Sct")
        emit_wc(64)
    emit_wc_flush()

    p.barrier()
    s3.close()

    s4 = ExitStack()
    a_xn4 = sb("p4_xn1", (128, NDM * T), s4)
    a_h1 = sb("p4_h1", (128, NFN * TC), s4, BF16)
    a_xe = sb("p4_xe", (128, 2 * NDM * TC), s4, BF16)
    a_w14b = sb("p4_w1b", (128, 2 * DM), s4, BF16)
    a_w24b = sb("p4_w2b", (128, 2 * DM), s4, BF16)
    a_mo = sb("p4_mo", (128, TC), s4)

    for dm in range(NDM):
        p.op("sync", lambda e, dm=dm: e.dma_start(
            out=a_xn4[:, dm * T:(dm + 1) * T].bitcast(F32R), in_=s_xn1[dm].bitcast(F32R)),
            r=(f"s_xn1{dm}",), w=(f"xn4_{dm}",), dma=True, dmasem="Lxn1")

    for e_ in range(NE):
        for tc4 in range(T // TC):
            sec = e_ * (T // TC) + tc4
            xs = sec % 2
            psDE = psD if xs == 0 else psE
            pde = "pD" if xs == 0 else "pE"
            xeS = a_xe[:, xs * NDM * TC:(xs + 1) * NDM * TC]
            p.op("tensor", MM(psDE[:, 0:TC], sb_sel[0:NE, e_ * 128:(e_ + 1) * 128],
                              sb_msk[0:NE, tc4 * TC:(tc4 + 1) * TC],
                              True, True), r=("msk", "const"), w=(pde,))
            for dm in range(NDM):
                p.op("vector", STT(xeS[:, dm * TC:(dm + 1) * TC],
                                   a_xn4[:, dm * T + tc4 * TC:dm * T + (tc4 + 1) * TC],
                                   0.0, psDE[:, 0:TC], Op.bypass, Op.mult),
                     r=(f"xn4_{dm}", pde), w=(f"xe{xs}",))
            for f in range(NFN):
                fp = f % 2
                blkb = a_w14b[:, fp * DM:(fp + 1) * DM]
                p.op("sync", lambda e, e_=e_, f=f, blkb=blkb: e.dma_start(
                    out=blkb, in_=s_W1b[e_][:, f * DM:(f + 1) * DM]),
                    r=("W1bD",), w=(f"w14b{fp}",), dma=True, dmasem=f"w14{fp}")
                pa = psA[:, fp * 512:fp * 512 + TC]
                for dm in range(NDM):
                    p.op("tensor", MMB(pa, blkb[:, dm * 128:(dm + 1) * 128],
                                       xeS[:, dm * TC:(dm + 1) * TC], dm == 0, dm == NDM - 1),
                         r=(f"w14b{fp}", f"xe{xs}"), w=(f"pA{fp}",))
                p.op("scalar", ACT(a_h1[:, f * TC:(f + 1) * TC], pa, AF.Gelu),
                     r=(f"pA{fp}",), w=("h1",))
            for dm in range(NDM):
                dp = dm % 2
                pb = psB[:, dp * 512:dp * 512 + TC]
                for qt in range(4):
                    hp = qt % 2
                    blk2b = a_w24b[:, hp * DM:(hp + 1) * DM]
                    p.op("sync", lambda e, e_=e_, dm=dm, qt=qt, blk2b=blk2b: e.dma_start(
                        out=blk2b, in_=s_W2b[e_][:, (dm * 4 + qt) * DM:(dm * 4 + qt + 1) * DM]),
                        r=("W2bD",), w=(f"w24b{hp}",), dma=True, dmasem=f"w24{hp}")
                    for fh in range(NFN // 4):
                        f = qt * (NFN // 4) + fh
                        p.op("tensor", MMB(pb, blk2b[:, fh * 128:(fh + 1) * 128],
                                           a_h1[:, f * TC:(f + 1) * TC], f == 0, f == NFN - 1),
                             r=(f"w24b{hp}", "h1"), w=(f"pB{dp}",))
                p.op("scalar", ACT(a_mo[:], pb, AF.Identity), r=(f"pB{dp}",), w=("mo",))
                p.op("gpsimd", lambda e, e_=e_, dm=dm, tc4=tc4: e.dma_start(
                    out=s_moe[dm, :, tc4 * TC:(tc4 + 1) * TC], in_=a_mo[:],
                    accum_op=(Op.add if e_ > 0 else Op.bypass)),
                    r=("mo",), w=(f"s_mo{dm}_{tc4}",), dma=True, dmasem="Smoe",
                    swdge=True)

    p.barrier()
    s4.close()

    s5 = ExitStack()
    a_x1f = sb("p5_x1", (128, NDM * T), s5)
    a_mof = sb("p5_mo", (128, NDM * T), s5)
    a_ctf = sb("p5_ct", (128, 2 * DM), s5)
    a_mot = sb("p5_mt", (128, 2 * 512), s5)
    a_of = sb("p5_of", (128, 2 * DM), s5)

    for dm in range(NDM):
        p.op("sync", lambda e, dm=dm: e.dma_start(
            out=a_x1f[:, dm * T:(dm + 1) * T], in_=s_x1[dm]),
            r=(f"s_x1{dm}",), w=(f"x1f_{dm}",), dma=True, dmasem="Lx1f")
        p.op("sync", lambda e, dm=dm: e.dma_start(
            out=a_mof[:, dm * T:(dm + 1) * T], in_=s_moe[dm]),
            r=tuple(f"s_mo{dm}_{t4}" for t4 in range(T // TC)),
            w=(f"mof_{dm}",), dma=True, dmasem="Lmof")
    if debug:
        for dm in range(NDM):
            p.op("gpsimd", lambda e, dm=dm: e.dma_start(
                out=dbg["d_moe"][dm], in_=a_mof[:, dm * T:(dm + 1) * T]),
                r=(f"mof_{dm}",), dma=True, dmasem="dbg")

    for ft in range(T // 128):
        par = ft % 2
        p.op("gpsimd", lambda e, ft=ft, par=par: e.dma_start(
            out=a_ctf[:, par * DM:(par + 1) * DM].bitcast(F32R), in_=s_ctxt[ft].bitcast(F32R)),
            r=(f"s_ct{ft}",), w=(f"ctf{par}",), dma=True, dmasem=f"Lct{par}")
        if debug:
            p.op("gpsimd", lambda e, ft=ft, par=par: e.dma_start(
                out=dbg["d_ctx"][ft], in_=a_ctf[:, par * DM:(par + 1) * DM]),
                r=(f"ctf{par}",), dma=True, dmasem="dbg")
        for g in range(2):
            pa = psA[:, par * 512:(par + 1) * 512]
            for j in range(4):
                dm = g * 4 + j
                p.op("tensor", TRA(pa[:, j * 128:(j + 1) * 128],
                                   a_mof[:, dm * T + ft * 128:dm * T + (ft + 1) * 128],
                                   j == 0, j == 3),
                     r=(f"mof_{dm}", "const"), w=(f"pA{par}",))
            p.op("vector", TS(R32(a_mot[:, par * 512:(par + 1) * 512]), pa, sb_mpt[:, ft:ft + 1],
                              Op.mult), r=(f"pA{par}", "mpt"), w=(f"mot{par}",))
            pb = psB[:, par * 512:(par + 1) * 512]
            for j in range(4):
                dm = g * 4 + j
                p.op("tensor", TRA(pb[:, j * 128:(j + 1) * 128],
                                   a_x1f[:, dm * T + ft * 128:dm * T + (ft + 1) * 128],
                                   j == 0, False),
                     r=(f"x1f_{dm}", "const"), w=(f"pB{par}",))
            p.op("tensor", MM(pb, sb_id[:], a_mot[:, par * 512:(par + 1) * 512], False, False),
                 r=(f"mot{par}", "const"), w=(f"pB{par}",))
            p.op("tensor", MM(pb, sb_id[:], a_ctf[:, par * DM + g * 512:par * DM + (g + 1) * 512],
                              False, True), r=(f"ctf{par}", "const"), w=(f"pB{par}",))
            p.op("scalar", ACT(a_of[:, par * DM + g * 512:par * DM + (g + 1) * 512], pb,
                               AF.Identity), r=(f"pB{par}",), w=(f"of{par}",))
        p.op("gpsimd", lambda e, ft=ft, par=par: e.dma_start(
            out=out_h[ft * 128:(ft + 1) * 128, :], in_=a_of[:, par * DM:(par + 1) * DM]),
            r=(f"of{par}",), w=("outd",), dma=True, dmasem="Sout")
    p.op("gpsimd", lambda e: e.nop(), r=("outd",))
    s5.close()

    if max_ops is not None:
        keep = p.all[:max_ops]
        p.all = keep
        kept = set(id(o) for o in keep)
        for e in ENGS:
            p.per_eng[e] = [o for o in p.per_eng[e] if id(o) in kept]
        for i, o in enumerate(p.all):
            o.i = i
        p.barrier()
    if mutate is not None:
        mutate(p)
    nsem, nsig = p.plan(es, sim_safe=sim_safe)
    with nc.allow_low_precision(reason="fp32r (fp22) operand production feeding fp32r matmuls"):
        with nc.Block() as block:
            p.emit(block)
    if debug:
        print(f"[build] ops={len(p.all)} sems={nsem} signaled={nsig}")
    return nc


_NC_CACHE = {}


def kernel(**inputs):
    if "nc" not in _NC_CACHE:
        _NC_CACHE["nc"] = build_program(sim_safe=True)
    nc = _NC_CACHE["nc"]
    f32 = lambda a: np.ascontiguousarray(np.asarray(a), dtype=np.float32)
    x = f32(inputs["L0_attn_x"])
    common = {
        "x_full": x.reshape(B * T, DM),
        "ln_gamma": f32(inputs["L0_attn_ln_gamma"]),
        "ln_beta": f32(inputs["L0_attn_ln_beta"]),
        "W_in": f32(inputs["L0_attn_W_in"]),
        "conv_w": f32(inputs["L0_attn_conv_w"]),
        "conv_b": f32(inputs["L0_attn_conv_b"]),
        "W_x": f32(inputs["L0_attn_W_x"]),
        "w_dt": f32(inputs["L0_attn_w_dt"]),
        "b_dt": f32(inputs["L0_attn_b_dt"]),
        "A_log": f32(inputs["L0_attn_A_log"]),
        "D_param": f32(inputs["L0_attn_D_param"]),
        "W_out": f32(inputs["L0_attn_W_out"]),
        "ffn_w": f32(inputs["L0_ffn_weight"]),
        "ffn_b": f32(inputs["L0_ffn_bias"]),
        "Wq": f32(inputs["L1_attn_Wq"]),
        "bq": f32(inputs["L1_attn_bq"]),
        "Wk": f32(inputs["L1_attn_Wk"]),
        "bk": f32(inputs["L1_attn_bk"]),
        "Wv": f32(inputs["L1_attn_Wv"]),
        "bv": f32(inputs["L1_attn_bv"]),
        "Wr": f32(inputs["L1_ffn_Wr"]),
        "br": f32(inputs["L1_ffn_br"]),
        "W1": f32(inputs["L1_ffn_W1"]),
        "W2": f32(inputs["L1_ffn_W2"]),
    }
    in_maps = []
    for c in range(B):
        m = dict(common)
        m["x_own"] = x[c]
        in_maps.append(m)
    res = run_bass_kernel_spmd(nc, in_maps, core_ids=list(range(B)))
    out = np.stack([np.asarray(r["out"]) for r in res.results], axis=0)
    return out.astype(np.float32)


# version 15
# speedup vs baseline: 1.2514x; 1.2514x over previous
import numpy as np

import concourse.bass as bass
import concourse.mybir as mybir
from concourse.alu_op_type import AluOpType as Op
from concourse.bass_utils import run_bass_kernel_spmd

F32 = mybir.dt.float32
F32R = mybir.dt.float32r
BF16 = mybir.dt.bfloat16
AF = mybir.ActivationFunctionType
AX = mybir.AxisListType.X

B = 8
T = 2048
DM = 1024
DI = 2048
DS = 16
DC = 4
FF = 4096
HN, HD = 16, 64
NE = 8
FFN = 4096
NDM, NDI, NF, NFN = DM // 128, DI // 128, FF // 128, FFN // 128
TC = 512
NCH = T // TC
NTT = TC // 128
LN_EPS = 1e-5
LN2 = float(np.log(2.0))

ENGS = ("sync", "tensor", "scalar", "vector", "gpsimd")
DMA_ENG = "sync"


class _Op:
    __slots__ = ("i", "eng", "fn", "r", "w", "dma", "sem", "val", "deps", "signal",
                 "fsig", "nosig", "_bar_deps")

    def __init__(self, i, eng, fn, r, w, dma, sem, fsig=False, nosig=False):
        self.i, self.eng, self.fn, self.r, self.w, self.dma, self.sem = i, eng, fn, r, w, dma, sem
        self.val = None
        self.deps = []
        self.signal = False
        self.fsig = fsig
        self.nosig = nosig


class Prog:

    def __init__(self, nc):
        self.nc = nc
        self.all = []
        self.per_eng = {e: [] for e in ENGS}
        self.fence_op = None

    def op(self, eng, fn, r=(), w=(), dma=False, dmasem=None, fsig=False, swdge=False,
           nosig=False):
        assert eng in ENGS
        if dma and eng == "gpsimd" and not swdge:
            eng = DMA_ENG
        r = tuple(r) + ("const",)
        sem = None
        if dma:
            assert dmasem is not None, "dma op needs dmasem key"
            sem = "D:" + dmasem
        o = _Op(len(self.all), eng, fn, tuple(r), tuple(w), dma, sem, fsig, nosig)
        self.all.append(o)
        self.per_eng[eng].append(o)
        return o

    def barrier(self):
        self.fence_op = ("BAR", None)
        o = self.op("gpsimd", None, r=("__bar__",), w=("__bar__",))
        o.fn = "__BARRIER__"
        return o

    def plan(self, es, sim_safe=True):
        last_writer = {}
        readers = {}
        bar_pending = None

        def rkey(o):
            return ("D", o.sem) if o.dma else ("E", o.eng)

        for o in self.all:
            deps = set()
            if o.fn == "__BARRIER__":
                bar_pending = o
                o._bar_deps = [q for q in self.all[:o.i]]
                continue
            if bar_pending is not None:
                deps.add(bar_pending)
            for rr in o.r:
                wN = last_writer.get(rr)
                if wN is not None:
                    deps.add(wN)
            for ww in o.w:
                wN = last_writer.get(ww)
                if wN is not None:
                    deps.add(wN)
                for rd in readers.get(ww, {}).values():
                    deps.add(rd)
            for ww in o.w:
                last_writer[ww] = o
                readers[ww] = {}
            for rr in o.r:
                readers.setdefault(rr, {})[rkey(o)] = o
            o.deps = [d for d in deps if d is not o]
        for o in self.all:
            if o.fn == "__BARRIER__":
                per_last = {}
                dma_deps = []
                for q in o._bar_deps:
                    if q.dma:
                        dma_deps.append(q)
                    else:
                        per_last[q.eng] = q
                for e, q in per_last.items():
                    assert not q.nosig, \
                        f"engine {e} last op before barrier is nosig (op {q.i})"
                o.deps = list(per_last.values()) + dma_deps
        for o in self.all:
            if o.fsig:
                o.signal = True
            for d in o.deps:
                if d.nosig and not d.dma:
                    assert d.eng == o.eng, \
                        f"cross-engine dep on nosig op {d.i} from {o.i}"
                    continue
                if sim_safe or d.dma or d.eng != o.eng or d.fsig:
                    d.signal = True
        sems = {}
        vals = {}

        def sem_of(o):
            if o.dma:
                return o.sem
            return "E:" + o.eng

        for o in self.all:
            if o.dma or o.signal or o.fn == "__BARRIER__":
                s = sem_of(o)
                if s not in sems:
                    sems[s] = es.enter_context(self.nc.semaphore(s.replace(":", "_")))
                    vals[s] = 0
                vals[s] += 16 if o.dma else 1
                o.val = (s, vals[s])
                if o.dma:
                    o.signal = True
        cum = {}
        cum_at = []
        snap = {}
        for o in self.all:
            cum_at.append(dict(snap) if o in () else None)
        run = {}
        idx_counts = [None] * len(self.all)
        for o in self.all:
            idx_counts[o.i] = dict(run) if any(
                (d.dma for d in o.deps)) or o.fn == "__BARRIER__" else None
            if o.dma:
                run[o.sem] = run.get(o.sem, 0) + 16
        self.dma_wait = idx_counts
        self.sems = sems
        nsig = sum(1 for o in self.all if o.signal)
        return len(sems), nsig

    def emit(self, block):
        def make(eng_name):
            def body(eng):
                seen = {}
                for o in self.per_eng[eng_name]:
                    for d in o.deps:
                        if d.val is None:
                            continue
                        if d.dma:
                            s = d.sem
                            v = self.dma_wait[o.i].get(s, 0)
                            if v <= 0:
                                continue
                        else:
                            s, v = d.val
                        if seen.get(s, -1) >= v:
                            continue
                        eng.wait_ge(self.sems[s], v)
                        seen[s] = v
                    if o.fn == "__BARRIER__":
                        if o.val is not None:
                            eng.sem_inc(self.sems[o.val[0]], 1)
                        continue
                    inst = o.fn(eng)
                    if o.signal:
                        assert inst is not None, f"signaled op on {eng_name} returned None"
                        inst.then_inc(self.sems[o.val[0]], 16 if o.dma else 1)
            return body

        for e in ENGS:
            getattr(block, e)(make(e))


def build_program(debug=False, sim_safe=False, max_ops=None, mutate=None):
    from contextlib import ExitStack
    es = ExitStack()
    nc = bass.Bass("TRN2", target_bir_lowering=False)
    p = Prog(nc)

    def din(name, shape):
        return nc.dram_tensor(name, list(shape), F32, kind="ExternalInput").ap()

    x_own = din("x_own", (T, DM))
    x_full = din("x_full", (B * T, DM))
    ln_g = din("ln_gamma", (DM,))
    ln_b = din("ln_beta", (DM,))
    W_in = din("W_in", (FF, DM))
    conv_w = din("conv_w", (DI, DC))
    conv_b = din("conv_b", (DI,))
    W_x = din("W_x", (1 + 2 * DS, DI))
    w_dt = din("w_dt", (DI, 1))
    b_dt = din("b_dt", (DI,))
    A_log = din("A_log", (DI, DS))
    D_par = din("D_param", (DI,))
    W_out = din("W_out", (DM, DI))
    f0_w = din("ffn_w", (DM, DM))
    f0_b = din("ffn_b", (DM,))
    Wq_h = din("Wq", (DM, DM))
    bq_h = din("bq", (DM,))
    Wk_h = din("Wk", (DM, DM))
    bk_h = din("bk", (DM,))
    Wv_h = din("Wv", (DM, DM))
    bv_h = din("bv", (DM,))
    Wr_h = din("Wr", (DM, NE))
    br_h = din("br", (NE,))
    W1_h = din("W1", (NE, DM, FFN))
    W2_h = din("W2", (NE, FFN, DM))
    out_h = nc.dram_tensor("out", [T, DM], F32, kind="ExternalOutput").ap()

    def dsc(name, shape, dt=F32):
        return nc.dram_tensor(name, list(shape), dt, kind="Internal").ap()

    s_WinT = dsc("s_WinT", (NF, 128, DM))
    s_WoutT = dsc("s_WoutT", (NDM, 128, DI))
    s_WqT = dsc("s_WqT", (NDM, 128, DM))
    s_WkT = dsc("s_WkT", (NDM, 128, DM))
    s_WvT = dsc("s_WvT", (NDM, 128, DM))
    s_Wf0T = dsc("s_Wf0T", (NDM, 128, DM))
    s_dt = dsc("s_dt", (NDI, 128, TC))
    s_xc = dsc("s_xc", (NDI, 128, TC))
    s_sz = dsc("s_sz", (NDI, 128, TC))
    s_x1 = dsc("s_x1", (NDM, 128, T))
    s_xn1 = dsc("s_xn1", (NDM, 128, T))
    s_moe = dsc("s_moe", (NDM, 128, T))
    s_ctxt = dsc("s_ctxt", (T // 128, 128, DM))
    s_W1b = dsc("s_W1b", (NE, 128, NFN * DM), BF16)
    s_W2b = dsc("s_W2b", (NE, 128, 32 * DM), BF16)

    dbg = {}
    if debug:
        for nm, shp in [("d_xn", (NDM, 128, T)), ("d_xn2", (NDM, 128, T)),
                        ("d_xq", (NDM, 128, T)), ("d_f0", (NDM, 128, T)),
                        ("d_xc", (NDI, 128, T)), ("d_dt", (NDI, 128, T)),
                        ("d_sz", (NDI, 128, T)), ("d_y2", (NDI, 128, T)),
                        ("d_x1", (NDM, 128, T)), ("d_xn1", (NDM, 128, T)),
                        ("d_sc", (1, 8)),
                        ("d_q", (NDM, 128, T)), ("d_k", (NDM, 128, T)),
                        ("d_v", (NDM, 128, T)),
                        ("d_ctx", (T // 128, 128, DM)),
                        ("d_moe", (NDM, 128, T)), ("d_rt", (NE, T)),
                        ("d_msk", (NE, T)), ("d_mp", (128, T // 128))]:
            dbg[nm] = nc.dram_tensor(nm, list(shp), F32, kind="ExternalOutput").ap()

    ident_h = nc.inline_tensor(np.eye(128, dtype=np.float32), name="c_ident").ap()
    ones_h = nc.inline_tensor(np.ones((1, 128), dtype=np.float32), name="c_onesr").ap()
    onesc_h = nc.inline_tensor(np.ones((128, 1), dtype=np.float32), name="c_onesc").ap()
    BIGK = 1e30
    CONSTS = [0.0, 1.0, -1.0, 0.5, float(LN_EPS), 1.0 / DM, 1.0 / LN2, -6.0, -30.0,
              127.0, 1.0 / 127.0, 1e-8, 1e-12, 448.0, 2.0 ** -30, 0.125, 1e30,
              LN2, -LN2, 3.0 * LN2, -3.0 * LN2, 7.0 * LN2, -7.0 * LN2,
              -9.0 * LN2, 9.0 * LN2, -31.0 * LN2, 31.0 * LN2]
    CONSTS += [-BIGK * float(c) for c in
               [1, 2, 4, 8, 16, 32, 64, 128, 256, 512]]
    consts_h = nc.inline_tensor(np.tile(np.array(CONSTS, dtype=np.float32), (128, 1)),
                                name="c_vals").ap()
    sel_np = np.zeros((DS, DS * 128), dtype=np.float32)
    for _n in range(DS):
        sel_np[_n, _n * 128:(_n + 1) * 128] = 1.0
    sel_h = nc.inline_tensor(sel_np, name="c_sel").ap()

    def sb(name, shape, stack=es, dt=F32):
        return stack.enter_context(nc.sbuf_tensor(name, list(shape), dt))

    def pst(name, shape, stack=es):
        return stack.enter_context(nc.psum_tensor(name, list(shape), F32))

    sb_id = sb("b_id", (128, 128))
    sb_on = sb("b_on", (1, 128))
    sb_onc = sb("b_onc", (128, 1))
    sb_cn = sb("b_cn", (128, len(CONSTS)))

    def C(v):
        return sb_cn[:, CONSTS.index(v):CONSTS.index(v) + 1]

    for _i, _v in enumerate(CONSTS):
        nc.const_aps.register(F32, _v, sb_cn[:, _i:_i + 1])

    sb_gam = sb("b_gam", (128, NDM))
    sb_bet = sb("b_bet", (128, NDM))
    sb_cw = sb("b_cw", (128, NDI * DC))
    sb_cb = sb("b_cb", (128, NDI))
    sb_bdt = sb("b_bdt", (128, NDI))
    sb_A = sb("b_A", (128, NDI * DS))
    sb_D = sb("b_D", (128, NDI))
    sb_f0b = sb("b_f0b", (128, NDM))
    sb_bqkv = sb("b_bqkv", (128, 3 * NDM))
    sb_br = sb("b_br", (NE, 1))
    sb_wdtr = sb("b_wdtr", (1, DI))
    sb_WxT = sb("b_WxT", (128, NDI * 33))
    sb_Wr = sb("b_Wr", (128, NDM * NE))
    sb_gs = sb("b_gs", (128, 8))
    sb_msk = sb("b_mskf", (NE, T))
    sb_mpt = sb("b_mpt", (128, T // 128))
    sb_z8 = sb("b_z8", (128, NE))
    sb_sel = sb("b_sel", (DS, DS * 128))
    sb_jnk = sb("b_jnk", (128, DM))

    psA = pst("psA", (128, 1024))
    psB = pst("psB", (128, 1024))
    psC = pst("psC", (128, 512))
    psD = pst("psD", (128, 512))
    psE = pst("psE", (128, 512))
    psF = pst("psF", (128, 512))

    def R32(ap):
        return ap.bitcast(F32R)

    def MMF32(out, lhsT, rhs, start, stop):
        return lambda e: e.matmul(out, lhsT, rhs, start=start, stop=stop)

    def MM(out, lhsT, rhs, start, stop):
        return lambda e: e.matmul(out, lhsT.bitcast(F32R), rhs.bitcast(F32R),
                                  start=start, stop=stop)

    def MMB(out, lhsT, rhs, start, stop):
        return lambda e: e.matmul(out, lhsT, rhs, start=start, stop=stop)

    def TRA(out, in_, start=True, stop=True):
        np_ = in_.shape[0]
        b = in_.base_partition()
        return lambda e: e.matmul(out, in_, sb_id[b:b + np_, b:b + np_],
                                  start=start, stop=stop, is_transpose=True)

    def ACT(out, in_, func, scale=None, bias=None, accum=None):
        kw = {}
        if scale is not None:
            kw["scale"] = scale
        if bias is not None:
            kw["bias"] = bias
        if accum is not None:
            kw["accum_out"] = accum
        return lambda e: e.activation(out=out, in_=in_, func=func, **kw)

    def TS(out, in0, s1, op0, s2=None, op1=None):
        if op1 is None:
            return lambda e: e.tensor_scalar(out=out, in0=in0, scalar1=s1,
                                             scalar2=None, op0=op0)
        return lambda e: e.tensor_scalar(out=out, in0=in0, scalar1=s1, scalar2=s2,
                                         op0=op0, op1=op1)

    def TT(out, in0, in1, op):
        return lambda e: e.tensor_tensor(out=out, in0=in0, in1=in1, op=op)

    def STT(out, in0, s, in1, op0, op1):
        return lambda e: e.scalar_tensor_tensor(out=out, in0=in0, scalar=s, in1=in1,
                                                op0=op0, op1=op1)

    def SCAN(out, a, b, init):
        return lambda e: e.tensor_tensor_scan(out=out, data0=a, data1=b, initial=init,
                                              op0=Op.mult, op1=Op.add)

    def CUMSUM(out, a, zeros, init):
        return lambda e: e.tensor_tensor_scan(out=out, data0=a, data1=zeros, initial=init,
                                              op0=Op.add, op1=Op.add)

    def CPY(out, in_):
        return lambda e: e.tensor_copy(out=out, in_=in_)

    def RMAX(out, in_, absv=False):
        return lambda e: e.reduce_max(out=out, in_=in_, axis=AX,
                                      apply_absolute_value=absv)

    def REC(out, in_):
        return lambda e: e.reciprocal(out=out, in_=in_)

    def MEMS(ap, v):
        return lambda e: e.memset(ap, v)

    def CMPGE(f_out, r_in, c, rr, wr):
        p.op("scalar", ACT(f_out, r_in, AF.Sigmoid, scale=C(BIGK), bias=C(-BIGK * float(c))),
             r=rr + ("const",), w=wr)

    def quant_grid(src, out, wA, wB, wC, wD, mant, eclamp, scale_col=None,
                   lnbias_col=None, rr=(), wr=(), out_r32=False):
        o, rA, rB, rC, rD = wr
        o = o if isinstance(o, tuple) else (o,)
        kmax = {(3, -6): 16, (7, -24): 64}[(mant, eclamp)]
        elevels = [kmax >> (i + 1) for i in range(kmax.bit_length() - 1)]
        C2 = float(2 ** (mant + 1))
        rlevels = [int(2 * C2) >> i for i in range(int(2 * C2).bit_length())]
        rlevels = [c for c in rlevels if c >= 1]
        if scale_col is not None:
            p.op("scalar", ACT(wB, src, AF.Abs, scale=scale_col), r=rr + ("gs",), w=(rB,))
        else:
            p.op("scalar", ACT(wB, src, AF.Abs), r=rr, w=(rB,))
        p.op("vector", TS(wB, wB, float(2.0 ** eclamp), Op.max), r=(rB,), w=(rB,))
        p.op("scalar", ACT(wB, wB, AF.Ln), r=(rB,), w=(rB,))
        p.op("vector", TS(wB, wB, 1.0 / LN2, Op.mult, -float(eclamp), Op.add),
             r=(rB,), w=(rB,))
        p.op("vector", CPY(wD, wB), r=(rB,), w=(rD,))
        for c in elevels:
            CMPGE(wC, wB, c, rr=(rB,), wr=(rC,))
            p.op("vector", STT(wB, wC, -float(c), wB, Op.mult, Op.add),
                 r=(rC, rB), w=(rB,))
        p.op("vector", TT(wD, wD, wB, Op.subtract), r=(rD, rB), w=(rD,))
        p.op("scalar", ACT(wA, wD, AF.Exp, scale=LN2,
                           bias=float(eclamp - mant) * LN2), r=(rD,), w=(rA,))
        if lnbias_col is not None:
            p.op("scalar", ACT(wD, wD, AF.Exp, scale=-LN2, bias=lnbias_col),
                 r=(rD, "gs"), w=(rD,))
        else:
            p.op("scalar", ACT(wD, wD, AF.Exp, scale=-LN2,
                               bias=float(mant - eclamp) * LN2), r=(rD,), w=(rD,))
        p.op("vector", TT(wD, src, wD, Op.mult), r=rr + (rD,), w=(rD,))
        p.op("vector", TS(wB, wD, 0.5 + C2, Op.add), r=(rD,), w=(rB,))
        p.op("vector", CPY(wD, wB), r=(rB,), w=(rD,))
        for c in rlevels:
            CMPGE(wC, wB, c, rr=(rB,), wr=(rC,))
            p.op("vector", STT(wB, wC, -float(c), wB, Op.mult, Op.add),
                 r=(rC, rB), w=(rB,))
        p.op("vector", TT(wD, wD, wB, Op.subtract), r=(rD, rB), w=(rD,))
        p.op("vector", STT(R32(out) if out_r32 else out, wD, -C2, wA, Op.add, Op.mult),
             r=(rD, rA), w=o)

    def round_int_ladder(t_in, out, wB, wC, wD, cmax, rr=(), wr=()):
        o, rB, rC, rD = wr
        C2 = float(2 ** (int(cmax + 1).bit_length()))
        rlevels = [int(2 * C2) >> i for i in range(int(2 * C2).bit_length())]
        rlevels = [c for c in rlevels if c >= 1]
        p.op("vector", TS(wB, t_in, 0.5 + C2, Op.add), r=rr, w=(rB,))
        p.op("vector", CPY(wD, wB), r=(rB,), w=(rD,))
        for c in rlevels:
            CMPGE(wC, wB, c, rr=(rB,), wr=(rC,))
            p.op("vector", STT(wB, wC, -float(c), wB, Op.mult, Op.add),
                 r=(rC, rB), w=(rB,))
        p.op("vector", TT(wD, wD, wB, Op.subtract), r=(rD, rB), w=(rD,))
        p.op("vector", TS(wD, wD, -C2, Op.add), r=(rD,), w=(rD,))
        p.op("vector", TS(out, wD, float(cmax), Op.min, -float(cmax), Op.max),
             r=(rD,), w=(o,))

    def ld(dst, src, key):
        def fn(e, dst=dst, src=src):
            with nc.allow_non_contiguous_dma(reason="tiny one-time param loads"):
                return e.dma_start(out=dst, in_=src)
        p.op("sync", fn, w=("const",), dma=True, dmasem=key)

    ld(sb_id[:].bitcast(F32R), ident_h[:].bitcast(F32R), "c")
    ld(sb_on[:].bitcast(F32R), ones_h[:].bitcast(F32R), "c")
    ld(sb_onc[:].bitcast(F32R), onesc_h[:].bitcast(F32R), "c")
    ld(sb_cn[:], consts_h[:], "c")
    ld(sb_gam[:], ln_g.rearrange("(c p) -> p c", p=128), "c")
    ld(sb_bet[:], ln_b.rearrange("(c p) -> p c", p=128), "c")
    ld(sb_cb[:], conv_b.rearrange("(c p) -> p c", p=128), "c")
    ld(sb_bdt[:], b_dt.rearrange("(c p) -> p c", p=128), "c")
    ld(sb_D[:], D_par.rearrange("(c p) -> p c", p=128), "c")
    ld(sb_f0b[:], f0_b.rearrange("(c p) -> p c", p=128), "c")
    ld(sb_bqkv[:, 0:NDM], bq_h.rearrange("(c p) -> p c", p=128), "c")
    ld(sb_bqkv[:, NDM:2 * NDM], bk_h.rearrange("(c p) -> p c", p=128), "c")
    ld(sb_bqkv[:, 2 * NDM:3 * NDM], bv_h.rearrange("(c p) -> p c", p=128), "c")
    ld(sb_br[:], br_h[:, None], "c")
    ld(sb_wdtr[:].bitcast(F32R), w_dt.rearrange("d one -> one d").bitcast(F32R), "c")
    ld(sb_cw[:].rearrange("p (c k) -> p c k", k=DC), conv_w.rearrange("(c p) k -> p c k", p=128), "c")
    ld(sb_A[:].rearrange("p (c n) -> p c n", n=DS), A_log.rearrange("(c p) n -> p c n", p=128), "c")
    ld(sb_Wr[:].rearrange("p (c e) -> p c e", e=NE).bitcast(F32R),
       Wr_h.rearrange("(c p) e -> p c e", p=128).bitcast(F32R), "c")
    ld(sb_sel[:].bitcast(F32R), sel_h[:].bitcast(F32R), "c")

    p.op("scalar", ACT(sb_A[:], sb_A[:], AF.Exp), r=("const",), w=("Acoef",))
    p.op("vector", TS(sb_A[:], sb_A[:], -1.0, Op.mult), r=("Acoef",), w=("Acoef",))
    p.op("vector", MEMS(sb_z8[:], 0.0), w=("z8",))

    s0 = ExitStack()
    a_x = sb("p0_x", (128, 8 * DM), s0)
    a_ab = sb("p0_ab", (128, 2 * DM), s0)
    a_st = sb("p0_st", (128, 96), s0)
    a_mx = sb("p0_mx", (128, 300), s0)
    a_sl = sb("p0_sl", (128, 2 * 2048), s0)
    a_sg = sb("p0_sg", (128, 2 * 2048), s0)
    a_q8 = sb("p0_q8", (128, 8 * DM), s0)
    a_w0 = sb("p0_w0", (128, DM), s0)
    a_w1 = sb("p0_w1", (128, DM), s0)
    a_w2 = sb("p0_w2", (128, DM), s0)
    a_w3 = sb("p0_w3", (128, DM), s0)
    wx_sb = sb("p0_wx", (33, DI), s0)
    a_gpk = sb("p0_gpk", (1, 8), s0)

    NTILE = B * T // 128
    GRP = 8
    for g in range(NTILE // GRP):
        for k in range(GRP):
            i = g * GRP + k
            p.op("sync", lambda e, i=i, k=k: e.dma_start(
                out=a_x[:, k * DM:(k + 1) * DM], in_=x_full[i * 128:(i + 1) * 128, :]),
                w=(f"p0x{k}",), dma=True, dmasem=f"p0x{k}")
        for k in range(GRP):
            p.op("scalar", ACT(a_ab[:, 0:DM], a_x[:, k * DM:(k + 1) * DM], AF.Identity,
                               accum=a_st[:, k:k + 1]),
                 r=(f"p0x{k}",), w=("p0ab0", "p0s"))
            p.op("scalar", ACT(a_ab[:, DM:2 * DM], a_x[:, k * DM:(k + 1) * DM], AF.Square,
                               accum=a_st[:, 16 + k:17 + k]),
                 r=(f"p0x{k}",), w=("p0ab1", "p0s"), fsig=True)
        S_, Q_ = a_st[:, 0:GRP], a_st[:, 16:16 + GRP]
        MU, VA = a_st[:, 32:32 + GRP], a_st[:, 48:48 + GRP]
        RS, NB = a_st[:, 64:64 + GRP], a_st[:, 80:80 + GRP]
        p.op("vector", TS(MU, S_, 1.0 / DM, Op.mult), r=("p0s", "const"), w=("p0m",))
        p.op("scalar", ACT(VA, MU, AF.Square), r=("p0m",), w=("p0v",))
        p.op("vector", STT(VA, Q_, 1.0 / DM, VA, Op.mult, Op.subtract), r=("p0s", "p0v"), w=("p0v",))
        p.op("scalar", ACT(VA, VA, AF.Sqrt, bias=float(LN_EPS)), r=("p0v",), w=("p0v",))
        p.op("vector", REC(RS, VA), r=("p0v",), w=("p0r",))
        p.op("vector", STT(NB, MU, -1.0, RS, Op.mult, Op.mult), r=("p0m", "p0r"), w=("p0n",))
        for k in range(GRP):
            i = g * GRP + k
            ab = a_ab[:, (k % 2) * DM:(k % 2 + 1) * DM]
            p.op("scalar", ACT(ab, a_x[:, k * DM:(k + 1) * DM], AF.Abs,
                               scale=RS[:, k:k + 1], bias=NB[:, k:k + 1]),
                 r=(f"p0x{k}", "p0r", "p0n"), w=(f"p0ab{k % 2}",))
            p.op("vector", RMAX(a_mx[:, i:i + 1], ab), r=(f"p0ab{k % 2}",), w=("p0mx",))
    p.op("vector", RMAX(a_mx[:, 128:129], a_mx[:, 0:128]), r=("p0mx",), w=("p0mx",))
    p.op("tensor", TRA(psC[0:1, 0:128], a_mx[:, 128:129]), r=("p0mx", "const"), w=("pC",))
    p.op("vector", CPY(a_mx[0:1, 129:257], psC[0:1, 0:128]), r=("pC",), w=("p0mx",))
    p.op("vector", RMAX(a_mx[0:1, 0:1], a_mx[0:1, 129:257]), r=("p0mx",), w=("p0am",))
    AMX = a_mx[0:1, 0:1]
    p.op("vector", TS(AMX, AMX, 1e-12, Op.max), r=("p0am", "const"), w=("p0am",))
    p.op("vector", REC(a_mx[0:1, 1:2], AMX), r=("p0am",), w=("p0am",))
    p.op("vector", TS(a_mx[0:1, 2:3], a_mx[0:1, 1:2], 448.0, Op.mult, 448.0, Op.min),
         r=("p0am", "const"), w=("p0am",))
    SX = a_mx[0:1, 2:3]

    for r in range(NDM):
        p.op("sync", lambda e, r=r: e.dma_start(
            out=a_q8[:, r * DM:(r + 1) * DM], in_=f0_w[r * 128:(r + 1) * 128, :]),
            w=(f"q8_{r}",), dma=True, dmasem="q8l")
    for r in range(NDM):
        p.op("vector", RMAX(a_mx[:, 10 + r:11 + r], a_q8[:, r * DM:(r + 1) * DM], absv=True),
             r=(f"q8_{r}",), w=("p0wm",))
    p.op("vector", RMAX(a_mx[:, 18:19], a_mx[:, 10:18]), r=("p0wm",), w=("p0wm",))
    p.op("tensor", TRA(psC[0:1, 128:256], a_mx[:, 18:19]), r=("p0wm", "const"), w=("pC",))
    p.op("vector", CPY(a_mx[0:1, 129:257], psC[0:1, 128:256]), r=("pC",), w=("p0wm",))
    p.op("vector", RMAX(a_mx[0:1, 19:20], a_mx[0:1, 129:257]), r=("p0wm",), w=("p0wm",))
    WMX = a_mx[0:1, 19:20]
    p.op("vector", TS(WMX, WMX, 1e-12, Op.max), r=("p0wm", "const"), w=("p0wm",))
    p.op("vector", REC(a_mx[0:1, 20:21], WMX), r=("p0wm",), w=("p0wm",))
    p.op("vector", TS(a_mx[0:1, 21:22], a_mx[0:1, 20:21], 448.0, Op.mult, 448.0, Op.min),
         r=("p0wm", "const"), w=("p0wm",))
    SW = a_mx[0:1, 21:22]
    p.op("vector", REC(a_mx[0:1, 4:5], SX), r=("p0am",), w=("p0g",))
    p.op("vector", REC(a_mx[0:1, 5:6], SW), r=("p0wm",), w=("p0g",))
    p.op("vector", TT(a_mx[0:1, 6:7], a_mx[0:1, 4:5], a_mx[0:1, 5:6], Op.mult),
         r=("p0g",), w=("p0g",))
    p.op("scalar", ACT(a_mx[0:1, 7:8], SX, AF.Ln), r=("p0am",), w=("p0g",))
    p.op("vector", TS(a_mx[0:1, 7:8], a_mx[0:1, 7:8], 9.0 * LN2, Op.add), r=("p0g",), w=("p0g",))
    p.op("scalar", ACT(a_mx[0:1, 8:9], SW, AF.Ln), r=("p0wm",), w=("p0g",))
    p.op("vector", TS(a_mx[0:1, 8:9], a_mx[0:1, 8:9], 9.0 * LN2, Op.add), r=("p0g",), w=("p0g",))
    p.op("vector", CPY(R32(a_gpk[0:1, 0:1]), a_mx[0:1, 6:7]), r=("p0g",), w=("p0g",))
    p.op("vector", CPY(R32(a_gpk[0:1, 1:2]), a_mx[0:1, 7:8]), r=("p0g",), w=("p0g",))
    p.op("vector", CPY(R32(a_gpk[0:1, 2:3]), SX), r=("p0am",), w=("p0g",))
    p.op("vector", CPY(R32(a_gpk[0:1, 3:4]), a_mx[0:1, 8:9]), r=("p0g",), w=("p0g",))
    p.op("vector", CPY(R32(a_gpk[0:1, 4:5]), SW), r=("p0wm",), w=("p0g",))
    p.op("tensor", MMF32(psD[:, 0:5], sb_on[:], a_gpk[0:1, 0:5], True, True),
         r=("p0g", "const"), w=("pD",))
    p.op("vector", CPY(sb_gs[:, 0:5], psD[:, 0:5]), r=("pD",), w=("gs",))
    GS_DEQ, GS_LNSX, GS_SX, GS_LNSW, GS_SW = (sb_gs[:, i:i + 1] for i in range(5))
    if debug:
        p.op("gpsimd", lambda e: e.dma_start(out=dbg["d_sc"][:], in_=a_mx[0:1, 0:8]),
             r=("p0g", "p0am"), dma=True, dmasem="dbg")

    for r in range(NDM):
        quant_grid(a_q8[:, r * DM:(r + 1) * DM], a_q8[:, r * DM:(r + 1) * DM],
                   a_w0[:], a_w1[:], a_w2[:], a_w3[:], mant=3, eclamp=-6,
                   scale_col=GS_SW, lnbias_col=GS_LNSW,
                   rr=(f"q8_{r}",), wr=(f"q8_{r}", "w0", "w1", "w2", "w3"))

    def transp_slabs(nrow, ncol, src_of, dst, skey, extra_r=()):
        for r in range(nrow):
            res, slab = src_of(r)
            stg = a_sg[:, (r % 2) * 2048:(r % 2) * 2048 + ncol * 128]
            for c in range(ncol):
                slot = c % 4
                p.op("tensor", TRA(psC[:, slot * 128:(slot + 1) * 128],
                                   slab[:, c * 128:(c + 1) * 128]),
                     r=(res, "const") + tuple(extra_r), w=("pC",))
                eng = "scalar" if c % 2 == 0 else "vector"
                dsta = stg[:, c * 128:(c + 1) * 128]
                fn = (ACT(dsta, psC[:, slot * 128:(slot + 1) * 128], AF.Identity)
                      if eng == "scalar" else CPY(dsta, psC[:, slot * 128:(slot + 1) * 128]))
                p.op(eng, fn, r=("pC",), w=(f"sg{r % 2}",))
            p.op("gpsimd", lambda e, stg=stg, r=r, dst=dst: e.dma_start(
                out=dst[r], in_=stg), r=(f"sg{r % 2}",), w=(skey,), dma=True, dmasem=skey)

    transp_slabs(NDM, NDM, lambda r: (f"q8_{r}", a_q8[:, r * DM:(r + 1) * DM]), s_Wf0T, "SWf0")

    def qkv_prep(W, dst, skey):
        def src_of(r):
            slab = a_sl[:, (r % 2) * 2048:(r % 2) * 2048 + DM]
            sres = f"sl{r % 2}"
            p.op("sync", lambda e, r=r, slab=slab, W=W: e.dma_start(
                out=slab, in_=W[r * 128:(r + 1) * 128, :]), w=(sres,), dma=True, dmasem=sres)
            rm = a_mx[:, 30 + (r % 2):31 + (r % 2)]
            rs = a_mx[:, 32 + (r % 2):33 + (r % 2)]
            rmr, rsr = f"rm{r % 2}", f"rs{r % 2}"
            p.op("vector", RMAX(rm, slab, absv=True), r=(sres,), w=(rmr,))
            p.op("vector", TS(rm, rm, 1.0 / 127.0, Op.mult, 1e-8, Op.max),
                 r=(rmr, "const"), w=(rmr,))
            p.op("vector", REC(rs, rm), r=(rmr,), w=(rsr,))
            p.op("vector", TS(a_w0[:], slab, rs, Op.mult), r=(sres, rsr), w=("w0",))
            round_int_ladder(a_w0[:], a_w0[:], a_w1[:], a_w2[:], a_w3[:], 127.0,
                             rr=("w0",), wr=("w0", "w1", "w2", "w3"))
            p.op("vector", TS(slab, a_w0[:], rm, Op.mult), r=("w0", rmr), w=(sres,))
            return (sres, slab)
        transp_slabs(NDM, NDM, src_of, dst, skey)

    qkv_prep(Wq_h, s_WqT, "SWq")
    qkv_prep(Wk_h, s_WkT, "SWk")
    qkv_prep(Wv_h, s_WvT, "SWv")

    def plain_prep(W, nrow, ncol, dst, skey):
        def src_of(r):
            slab = a_sl[:, (r % 2) * 2048:(r % 2) * 2048 + ncol * 128]
            sres = f"sl{r % 2}"
            p.op("sync", lambda e, r=r, slab=slab, W=W: e.dma_start(
                out=slab, in_=W[r * 128:(r + 1) * 128, :]), w=(sres,), dma=True, dmasem=sres)
            return (sres, slab)
        transp_slabs(nrow, ncol, src_of, dst, skey)

    plain_prep(W_in, NF, NDM, s_WinT, "SWi")
    plain_prep(W_out, NDM, NDI, s_WoutT, "SWo")

    p.op("sync", lambda e: e.dma_start(out=wx_sb[:], in_=W_x[:]), w=("wx",), dma=True, dmasem="wx")
    for c in range(NDI):
        slot = c % 4
        p.op("tensor", TRA(psC[:, slot * 128:slot * 128 + 33], wx_sb[:, c * 128:(c + 1) * 128]),
             r=("wx", "const"), w=("pC",))
        p.op("vector", CPY(R32(sb_WxT[:, c * 33:(c + 1) * 33]), psC[:, slot * 128:slot * 128 + 33]),
             r=("pC",), w=("WxT",))

    p.barrier()
    s0.close()

    s1 = ExitStack()
    a_xt = sb("p1_xt", (128, NTT * DM), s1)
    a_xnd = sb("p1_xnd", (128, NDM * TC), s1)
    a_xn2 = sb("p1_xn2", (128, NDM * TC), s1)
    a_xb = sb("p1_xb", (128, 2 * (TC + 8)), s1)
    a_y2 = sb("p1_y2", (128, NDI * TC), s1)
    a_h = sb("p1_h", (128, DS * TC), s1)
    a_sc = sb("p1_sc", (128, 10 * TC), s1)
    a_stt = sb("p1_st", (128, 64), s1)
    a_hst = sb("p1_hst", (128, NDI * DS), s1)
    a_hal = sb("p1_hal", (128, NDI * 3), s1)
    a_r2 = sb("p1_r2", (1, 2 * TC), s1)
    a_B = sb("p1_B", (DS, TC), s1)
    a_C = sb("p1_C", (DS, TC), s1)
    a_dtr = sb("p1_dtr", (1, TC), s1)
    a_r1 = sb("p1_r1", (1, 2 * TC), s1)
    a_wst = sb("p1_wst", (128, 2 * DM), s1)
    a_wso = sb("p1_wso", (128, 2 * DM), s1)
    a_tl = sb("p1_tl", (128, 3 * TC), s1)

    SCN = {"a0": 0, "a1": 1, "b0": 2, "b1": 3, "u": 4, "dt": 5, "xc": 6, "sz": 7,
           "t0": 8, "t1": 9, "p0": 2, "p1": 3}

    def scr(nm):
        i = SCN[nm]
        return a_sc[:, i * TC:(i + 1) * TC]

    def sc(nm):
        return scr(nm).bitcast(F32R)

    p.op("vector", MEMS(a_hst[:], 0.0), w=("hst",))
    p.op("vector", MEMS(a_hal[:], 0.0), w=("halo",))

    for ch in range(NCH):
        t0 = ch * TC
        for tt in range(NTT):
            p.op("sync", lambda e, tt=tt, t0=t0: e.dma_start(
                out=a_xt[:, tt * DM:(tt + 1) * DM],
                in_=x_own[t0 + tt * 128:t0 + (tt + 1) * 128, :]),
                w=(f"xt{tt}",), dma=True, dmasem=f"xt{tt}")
        for tt in range(NTT):
            p.op("scalar", ACT(sb_jnk[:], a_xt[:, tt * DM:(tt + 1) * DM], AF.Identity,
                               accum=a_stt[:, tt:tt + 1]), r=(f"xt{tt}",), w=("jnk", "st1"), fsig=True)
            p.op("scalar", ACT(sb_jnk[:], a_xt[:, tt * DM:(tt + 1) * DM], AF.Square,
                               accum=a_stt[:, 4 + tt:5 + tt]), r=(f"xt{tt}",), w=("jnk", "st1"), fsig=True)
        MU, VA = a_stt[:, 8:12], a_stt[:, 12:16]
        RS, NB = a_stt[:, 16:20], a_stt[:, 20:24]
        p.op("vector", TS(MU, a_stt[:, 0:4], 1.0 / DM, Op.mult), r=("st1", "const"), w=("st1m",))
        p.op("scalar", ACT(VA, MU, AF.Square), r=("st1m",), w=("st1v",))
        p.op("vector", STT(VA, a_stt[:, 4:8], 1.0 / DM, VA, Op.mult, Op.subtract),
             r=("st1", "st1v"), w=("st1v",))
        p.op("scalar", ACT(VA, VA, AF.Sqrt, bias=float(LN_EPS)), r=("st1v",), w=("st1v",))
        p.op("vector", REC(RS, VA), r=("st1v",), w=("st1r",))
        p.op("vector", STT(NB, MU, -1.0, RS, Op.mult, Op.mult), r=("st1m", "st1r"), w=("st1n",))
        for tt in range(NTT):
            p.op("scalar", ACT(R32(a_xn2[:, tt * DM:(tt + 1) * DM]), a_xt[:, tt * DM:(tt + 1) * DM],
                               AF.Identity, scale=RS[:, tt:tt + 1], bias=NB[:, tt:tt + 1]),
                 r=(f"xt{tt}", "st1r", "st1n"), w=(f"xnt{tt}", "xn2"))
            p.op("scalar", ACT(sb_jnk[:], a_xn2[:, tt * DM:(tt + 1) * DM], AF.Identity,
                               accum=a_stt[:, 24 + tt:25 + tt]), r=(f"xnt{tt}",), w=("jnk", "st2"), fsig=True)
            p.op("scalar", ACT(sb_jnk[:], a_xn2[:, tt * DM:(tt + 1) * DM], AF.Square,
                               accum=a_stt[:, 28 + tt:29 + tt]), r=(f"xnt{tt}",), w=("jnk", "st2"), fsig=True)
        MU2, VA2 = a_stt[:, 32:36], a_stt[:, 36:40]
        p.op("vector", TS(MU2, a_stt[:, 24:28], 1.0 / DM, Op.mult), r=("st2", "const"), w=("st2m",))
        p.op("scalar", ACT(VA2, MU2, AF.Square), r=("st2m",), w=("st2v",))
        p.op("vector", STT(VA2, a_stt[:, 28:32], 1.0 / DM, VA2, Op.mult, Op.subtract),
             r=("st2", "st2v"), w=("st2v",))
        p.op("scalar", ACT(VA2, VA2, AF.Sqrt, bias=float(LN_EPS)), r=("st2v",), w=("st2v",))
        RS2 = a_stt[:, 24:28]
        p.op("vector", REC(RS2, VA2), r=("st2v",), w=("st2r",))
        M2R = a_stt[:, 28:32]
        p.op("vector", TT(M2R, MU2, RS2, Op.mult), r=("st2m", "st2r"), w=("st2n",))
        for tt in range(NTT):
            p.op("tensor", TRA(psC[0:1, 0:128], a_stt[:, 24 + tt:25 + tt]),
                 r=("st2r", "const"), w=("pC",))
            p.op("vector", CPY(R32(a_r2[0:1, tt * 128:(tt + 1) * 128]), psC[0:1, 0:128]),
                 r=("pC",), w=("r2",))
            p.op("tensor", TRA(psC[0:1, 128:256], a_stt[:, 28 + tt:29 + tt]),
                 r=("st2n", "const"), w=("pC",))
            p.op("vector", CPY(R32(a_r2[0:1, TC + tt * 128:TC + (tt + 1) * 128]), psC[0:1, 128:256]),
                 r=("pC",), w=("r2",))
        p.op("tensor", MM(psD[:, 0:TC], sb_on[:], a_r2[0:1, 0:TC], True, True),
             r=("r2", "const"), w=("pD",))
        p.op("tensor", MM(psE[:, 0:TC], sb_on[:], a_r2[0:1, TC:2 * TC], True, True),
             r=("r2", "const"), w=("pE",))

        for tt in range(NTT):
            for dm in range(NDM):
                slot = (tt * NDM + dm) % 4
                p.op("tensor", TRA(psC[:, slot * 128:(slot + 1) * 128],
                                   a_xn2[:, tt * DM + dm * 128:tt * DM + (dm + 1) * 128]),
                     r=(f"xnt{tt}", "const"), w=("pC",))
                dsta = R32(a_xnd[:, dm * TC + tt * 128:dm * TC + (tt + 1) * 128])
                eng = "scalar" if (tt * NDM + dm) % 2 == 0 else "vector"
                fn = (ACT(dsta, psC[:, slot * 128:(slot + 1) * 128], AF.Identity)
                      if eng == "scalar" else CPY(dsta, psC[:, slot * 128:(slot + 1) * 128]))
                p.op(eng, fn, r=("pC",), w=("xnd",))
        if debug:
            for dm in range(NDM):
                p.op("gpsimd", lambda e, dm=dm, t0=t0: e.dma_start(
                    out=dbg["d_xn"][dm, :, t0:t0 + TC], in_=a_xnd[:, dm * TC:(dm + 1) * TC]),
                    r=("xnd",), dma=True, dmasem="dbg")

        for dm in range(NDM):
            p.op("vector", STT(sc("t0"), a_xnd[:, dm * TC:(dm + 1) * TC], 0.0,
                               psD[:, 0:TC], Op.bypass, Op.mult), r=("xnd", "pD"), w=("sct0",))
            p.op("vector", STT(sc("t0"), sc("t0"), 0.0, psE[:, 0:TC], Op.bypass, Op.subtract),
                 r=("sct0", "pE"), w=("sct0",))
            p.op("scalar", ACT(R32(a_xn2[:, dm * TC:(dm + 1) * TC]), sc("t0"), AF.Identity,
                               scale=sb_gam[:, dm:dm + 1], bias=sb_bet[:, dm:dm + 1]),
                 r=("sct0", "const"), w=("xn2", f"xnt{dm // 2}"))
        if debug:
            for dm in range(NDM):
                p.op("gpsimd", lambda e, dm=dm, t0=t0: e.dma_start(
                    out=dbg["d_xn2"][dm, :, t0:t0 + TC], in_=a_xn2[:, dm * TC:(dm + 1) * TC]),
                    r=("xn2",), dma=True, dmasem="dbg")

        for f in range(NF):
            fp = f % 2
            blk = a_wst[:, fp * DM:(fp + 1) * DM]
            p.op("sync", lambda e, f=f, blk=blk: e.dma_start(
                out=blk.bitcast(F32R), in_=s_WinT[f].bitcast(F32R)),
                 r=("SWi",), w=(f"wst{fp}",), dma=True, dmasem=f"wst{fp}")
            pa = psA[:, fp * 512:fp * 512 + TC]
            for dm in range(NDM):
                p.op("tensor", MM(pa, blk[:, dm * 128:(dm + 1) * 128],
                                  a_xn2[:, dm * TC:(dm + 1) * TC], dm == 0, dm == NDM - 1),
                     r=(f"wst{fp}", "xn2"), w=(f"pA{fp}",))
            if f < NDI:
                di = f
                xbb = a_xb[:, fp * (TC + 8):fp * (TC + 8) + TC + 8]
                p.op("scalar", ACT(xbb[:, 3:3 + TC], pa, AF.Identity),
                     r=(f"pA{fp}",), w=(f"xb{fp}",))
                p.op("vector", CPY(xbb[:, 0:3], a_hal[:, di * 3:(di + 1) * 3]),
                     r=("halo",), w=(f"xb{fp}",))
                p.op("vector", TS(sc("t1"), xbb[:, 0:TC], sb_cw[:, di * DC:di * DC + 1], Op.mult),
                     r=(f"xb{fp}", "const"), w=("sct1",))
                for k in range(1, DC):
                    p.op("vector", STT(sc("t1"), xbb[:, k:k + TC],
                                       sb_cw[:, di * DC + k:di * DC + k + 1],
                                       sc("t1"), Op.mult, Op.add),
                         r=(f"xb{fp}", "sct1", "const"), w=("sct1",))
                p.op("vector", CPY(a_hal[:, di * 3:(di + 1) * 3], xbb[:, TC:TC + 3]),
                     r=(f"xb{fp}",), w=("halo",))
                p.op("scalar", ACT(R32(sc("xc")), sc("t1"), AF.Silu, bias=sb_cb[:, di:di + 1]),
                     r=("sct1", "const"), w=("scxc",))
                p.op("gpsimd", lambda e, di=di: e.dma_start(out=s_xc[di], in_=scr("xc")),
                     r=("scxc",), w=(f"s_xc{di}",), dma=True, dmasem="Sxc")
                p.op("tensor", MM(psF[0:1, 0:TC], sb_WxT[:, di * 33:di * 33 + 1],
                                  sc("xc"), di == 0, di == NDI - 1),
                     r=("scxc", "WxT"), w=("pF",))
                p.op("tensor", MM(psD[0:DS, 0:TC], sb_WxT[:, di * 33 + 1:di * 33 + 1 + DS],
                                  sc("xc"), di == 0, di == NDI - 1),
                     r=("scxc", "WxT"), w=("pD",))
                p.op("tensor", MM(psE[0:DS, 0:TC], sb_WxT[:, di * 33 + 1 + DS:di * 33 + 33],
                                  sc("xc"), di == 0, di == NDI - 1),
                     r=("scxc", "WxT"), w=("pE",))
            else:
                di = f - NDI
                p.op("scalar", ACT(sc("sz"), pa, AF.Silu), r=(f"pA{fp}",), w=("scsz",))
                p.op("gpsimd", lambda e, di=di: e.dma_start(out=s_sz[di], in_=scr("sz")),
                     r=("scsz",), w=(f"s_sz{di}",), dma=True, dmasem="Ssz")

        p.op("scalar", ACT(R32(a_dtr[:]), psF[0:1, 0:TC], AF.Identity), r=("pF",), w=("dtr",))
        p.op("scalar", ACT(R32(a_B[:]), psD[0:DS, 0:TC], AF.Identity), r=("pD",), w=("Brow",))
        p.op("scalar", ACT(R32(a_C[:]), psE[0:DS, 0:TC], AF.Identity), r=("pE",), w=("Crow",))
        for di in range(NDI):
            p.op("tensor", MM(psC[:, 0:TC], sb_wdtr[0:1, di * 128:(di + 1) * 128],
                              a_dtr[0:1, 0:TC], True, True),
                 r=("dtr", "const"), w=("pC", "pC", "pC", "pC"))
            p.op("scalar", ACT(sc("t0"), psC[:, 0:TC], AF.Abs, bias=sb_bdt[:, di:di + 1]),
                 r=("pC", "pC", "pC", "pC", "const"), w=("sct0",))
            p.op("scalar", ACT(sc("t0"), sc("t0"), AF.Exp, scale=-1.0), r=("sct0",), w=("sct0",))
            p.op("scalar", ACT(sc("t0"), sc("t0"), AF.Ln, bias=1.0), r=("sct0",), w=("sct0",))
            p.op("scalar", ACT(sc("t1"), psC[:, 0:TC], AF.Relu, bias=sb_bdt[:, di:di + 1]),
                 r=("pC", "pC", "pC", "pC", "const"), w=("sct1",))
            p.op("vector", TT(sc("dt"), sc("t1"), sc("t0"), Op.add), r=("sct0", "sct1"), w=("scdt",))
            p.op("gpsimd", lambda e, di=di: e.dma_start(out=s_dt[di], in_=scr("dt")),
                 r=("scdt",), w=(f"s_dt{di}",), dma=True, dmasem="Sdt")

        for di in range(NDI):
            p.op("gpsimd", lambda e, di=di: e.dma_start(out=sc("dt"), in_=s_dt[di].bitcast(F32R)),
                 r=(f"s_dt{di}",), w=("scdt",), dma=True, dmasem="Ldt")
            p.op("gpsimd", lambda e, di=di: e.dma_start(out=sc("xc"), in_=s_xc[di].bitcast(F32R)),
                 r=(f"s_xc{di}",), w=("scxc",), dma=True, dmasem="Lxc")
            p.op("vector", TT(sc("u"), sc("dt"), sc("xc"), Op.mult), r=("scdt", "scxc"), w=("scu",))
            for n in range(DS):
                par = n % 2
                ab, bb = sc(f"a{par}"), sc(f"b{par}")
                p.op("tensor", MM(psD[:, 0:TC], sb_sel[:, n * 128:(n + 1) * 128],
                                  a_B[:, 0:TC], True, True),
                     r=("Brow", "const"), w=("pD",))
                p.op("tensor", MM(psE[:, 0:TC], sb_sel[:, n * 128:(n + 1) * 128],
                                  a_C[:, 0:TC], True, True),
                     r=("Crow", "const"), w=("pE",))
                p.op("scalar", ACT(ab, sc("dt"), AF.Exp,
                                   scale=sb_A[:, di * DS + n:di * DS + n + 1]),
                     r=("scdt", "Acoef"), w=(f"sca{par}",))
                p.op("vector", STT(bb, sc("u"), 0.0, psD[:, 0:TC], Op.bypass, Op.mult),
                     r=("scu", "pD"), w=(f"scb{par}",))
                hsl = a_h[:, n * TC:(n + 1) * TC]
                p.op("vector", SCAN(hsl, ab, bb, a_hst[:, di * DS + n:di * DS + n + 1]),
                     r=(f"sca{par}", f"scb{par}", "hst"), w=(f"h{n}",))
                pp = sc(f"p{par}")
                p.op("vector", STT(R32(pp), hsl, 0.0, psE[:, 0:TC], Op.bypass, Op.mult),
                     r=(f"h{n}", "pE"), w=(f"scb{par}",))
                p.op("tensor", MM(psF[:, 0:TC], sb_id[:], pp, n == 0, n == DS - 1),
                     r=(f"scb{par}", "const"), w=("pF",))
            p.op("vector", CPY(a_hst[:, di * DS:(di + 1) * DS],
                               a_h[:].rearrange("q (n t) -> q n t", n=DS)[:, :, TC - 1]),
                 r=tuple(f"h{n}" for n in range(DS)), w=("hst",))
            p.op("gpsimd", lambda e, di=di: e.dma_start(out=sc("sz"), in_=s_sz[di].bitcast(F32R)),
                 r=(f"s_sz{di}",), w=("scsz",), dma=True, dmasem="Lsz")
            p.op("vector", STT(sc("t1"), sc("xc"), sb_D[:, di:di + 1], psF[:, 0:TC],
                               Op.mult, Op.add), r=("scxc", "pF", "const"), w=("sct1",))
            p.op("vector", TT(R32(a_y2[:, di * TC:(di + 1) * TC]), sc("t1"), sc("sz"), Op.mult),
                 r=("sct1", "scsz"), w=("y2",))
            if debug:
                p.op("gpsimd", lambda e, di=di, t0=t0: e.dma_start(
                    out=dbg["d_y2"][di, :, t0:t0 + TC], in_=a_y2[:, di * TC:(di + 1) * TC]),
                    r=("y2",), dma=True, dmasem="dbg")
                p.op("gpsimd", lambda e, di=di, t0=t0: e.dma_start(
                    out=dbg["d_dt"][di, :, t0:t0 + TC], in_=scr("dt")),
                    r=("scdt",), dma=True, dmasem="dbg")
                p.op("gpsimd", lambda e, di=di, t0=t0: e.dma_start(
                    out=dbg["d_xc"][di, :, t0:t0 + TC], in_=scr("xc")),
                    r=("scxc",), dma=True, dmasem="dbg")
                p.op("gpsimd", lambda e, di=di, t0=t0: e.dma_start(
                    out=dbg["d_sz"][di, :, t0:t0 + TC], in_=scr("sz")),
                    r=("scsz",), dma=True, dmasem="dbg")

        for dm in range(NDM):
            quant_grid(a_xnd[:, dm * TC:(dm + 1) * TC], a_xn2[:, dm * TC:(dm + 1) * TC],
                       sc("t0"), sc("t1"), sc("p0"), sc("p1"), mant=3, eclamp=-6,
                       scale_col=GS_SX, lnbias_col=GS_LNSX, out_r32=True,
                       rr=("xnd",), wr=(("xn2", f"xnt{dm // 2}"), "sct0", "sct1", "scb0", "scb1"))
        if debug:
            for dm in range(NDM):
                p.op("gpsimd", lambda e, dm=dm, t0=t0: e.dma_start(
                    out=dbg["d_xq"][dm, :, t0:t0 + TC], in_=a_xn2[:, dm * TC:(dm + 1) * TC]),
                    r=("xn2",), dma=True, dmasem="dbg")

        for dm in range(NDM):
            fp = dm % 2
            blkq = a_wst[:, fp * DM:(fp + 1) * DM]
            p.op("sync", lambda e, dm=dm, blkq=blkq: e.dma_start(
                out=blkq.bitcast(F32R), in_=s_Wf0T[dm].bitcast(F32R)),
                 r=("SWf0",), w=(f"wst{fp}",), dma=True, dmasem=f"wst{fp}")
            for d2 in range(NDM):
                p.op("tensor", MM(psB[:, 0:TC], blkq[:, d2 * 128:(d2 + 1) * 128],
                                  a_xn2[:, d2 * TC:(d2 + 1) * TC], d2 == 0, d2 == NDM - 1),
                     r=(f"wst{fp}", "xn2"), w=("pB0",))
            p.op("scalar", ACT(R32(a_tl[:, 0:TC]), psB[:, 0:TC], AF.Identity,
                               scale=GS_DEQ, bias=sb_f0b[:, dm:dm + 1]),
                 r=("pB0", "gs", "const"), w=("tl0",))
            quant_grid(a_tl[:, 0:TC], a_tl[:, 0:TC], sc("t0"), sc("t1"), sc("p0"), sc("p1"),
                       mant=7, eclamp=-24, out_r32=True,
                       rr=("tl0",), wr=("tl0", "sct0", "sct1", "scb0", "scb1"))
            if debug:
                p.op("gpsimd", lambda e, dm=dm, t0=t0: e.dma_start(
                    out=dbg["d_f0"][dm, :, t0:t0 + TC], in_=a_tl[:, 0:TC]),
                    r=("tl0",), dma=True, dmasem="dbg")
            pb2 = psB[:, 512:512 + TC]
            for hf in range(2):
                blko = a_wso[:, hf * DM:(hf + 1) * DM]
                p.op("sync", lambda e, dm=dm, hf=hf, blko=blko: e.dma_start(
                    out=blko.bitcast(F32R), in_=s_WoutT[dm][:, hf * DM:(hf + 1) * DM].bitcast(F32R)),
                    r=("SWo",), w=(f"wso{hf}",), dma=True, dmasem=f"wso{hf}")
                for dj in range(NDM):
                    di = hf * NDM + dj
                    p.op("tensor", MM(pb2, blko[:, dj * 128:(dj + 1) * 128],
                                      a_y2[:, di * TC:(di + 1) * TC], di == 0, False),
                         r=(f"wso{hf}", "y2"), w=("pB1",))
            p.op("tensor", MM(pb2, sb_id[:], a_xnd[:, dm * TC:(dm + 1) * TC], False, False),
                 r=("xnd", "const"), w=("pB1",))
            p.op("tensor", MM(pb2, sb_id[:], a_tl[:, 0:TC], False, False),
                 r=("tl0", "const"), w=("pB1",))
            for tt in range(NTT):
                p.op("tensor", TRA(pb2.rearrange("q (n f) -> q n f", n=NTT)[:, tt],
                                   a_xt[:, tt * DM + dm * 128:tt * DM + (dm + 1) * 128],
                                   False, tt == NTT - 1),
                     r=(f"xt{tt}", "const"), w=("pB1",))
            p.op("scalar", ACT(R32(a_tl[:, TC:2 * TC]), pb2, AF.Identity), r=("pB1",), w=("tl1",))
            p.op("gpsimd", lambda e, dm=dm, t0=t0: e.dma_start(
                out=s_x1[dm, :, t0:t0 + TC], in_=a_tl[:, TC:2 * TC]),
                r=("tl1",), w=(f"s_x1{dm}",), dma=True, dmasem="Sx1")
            p.op("scalar", ACT(R32(a_tl[:, 2 * TC:3 * TC]), pb2, AF.Square), r=("pB1",), w=("tl2",))
            p.op("tensor", MM(psF[0:1, 0:TC], sb_onc[:], a_tl[:, TC:2 * TC], dm == 0, dm == NDM - 1),
                 r=("tl1", "const"), w=("pF",))
            p.op("tensor", MM(psD[0:1, 0:TC], sb_onc[:], a_tl[:, 2 * TC:3 * TC], dm == 0, dm == NDM - 1),
                 r=("tl2", "const"), w=("pD",))

        R = a_r1
        p.op("vector", TS(R32(R[0:1, 0:TC]), psF[0:1, 0:TC], 1.0 / DM, Op.mult),
             r=("pF", "const"), w=("r1",))
        p.op("scalar", ACT(R32(R[0:1, TC:2 * TC]), R[0:1, 0:TC], AF.Square), r=("r1",), w=("r1b",))
        p.op("vector", STT(R32(R[0:1, TC:2 * TC]), psD[0:1, 0:TC], 1.0 / DM, R[0:1, TC:2 * TC],
                           Op.mult, Op.subtract), r=("pD", "r1b"), w=("r1b",))
        p.op("scalar", ACT(R32(R[0:1, TC:2 * TC]), R[0:1, TC:2 * TC], AF.Sqrt, bias=float(LN_EPS)),
             r=("r1b",), w=("r1b",))
        p.op("vector", REC(R32(R[0:1, TC:2 * TC]), R[0:1, TC:2 * TC]), r=("r1b",), w=("r1b",))
        p.op("vector", TT(R32(R[0:1, 0:TC]), R[0:1, 0:TC], R[0:1, TC:2 * TC], Op.mult),
             r=("r1", "r1b"), w=("r1",))
        p.op("tensor", MM(psD[:, 0:TC], sb_on[:], R[0:1, TC:2 * TC], True, True),
             r=("r1b", "const"), w=("pD",))
        p.op("tensor", MM(psE[:, 0:TC], sb_on[:], R[0:1, 0:TC], True, True),
             r=("r1", "const"), w=("pE",))
        for dm in range(NDM):
            p.op("gpsimd", lambda e, dm=dm, t0=t0: e.dma_start(
                out=a_tl[:, TC:2 * TC], in_=s_x1[dm, :, t0:t0 + TC]),
                r=(f"s_x1{dm}",), w=("tl1",), dma=True, dmasem="Lx1")
            p.op("vector", STT(R32(a_tl[:, 0:TC]), a_tl[:, TC:2 * TC], 0.0, psD[:, 0:TC],
                               Op.bypass, Op.mult), r=("tl1", "pD"), w=("tl0",))
            p.op("vector", STT(R32(a_tl[:, 0:TC]), a_tl[:, 0:TC], 0.0, psE[:, 0:TC],
                               Op.bypass, Op.subtract), r=("tl0", "pE"), w=("tl0",))
            p.op("gpsimd", lambda e, dm=dm, t0=t0: e.dma_start(
                out=s_xn1[dm, :, t0:t0 + TC], in_=a_tl[:, 0:TC]),
                r=("tl0",), w=(f"s_xn1{dm}",), dma=True, dmasem="Sxn1")
            p.op("tensor", MM(psC[0:NE, 0:TC], sb_Wr[:, dm * NE:(dm + 1) * NE],
                              a_tl[:, 0:TC], dm == 0, dm == NDM - 1),
                 r=("tl0", "const"), w=("pC", "pC", "pC", "pC"))
        p.op("scalar", ACT(R32(a_B[0:NE, 0:TC]), psC[0:NE, 0:TC], AF.Sigmoid, bias=sb_br[:]),
             r=("pC", "pC", "pC", "pC", "const"), w=("Brow",))
        if debug:
            p.op("gpsimd", lambda e, t0=t0: e.dma_start(
                out=dbg["d_rt"][:, t0:t0 + TC], in_=a_B[0:NE, 0:TC]),
                r=("Brow",), dma=True, dmasem="dbg")
        for tt in range(NTT):
            p.op("tensor", TRA(psC[:, 384:392], a_B[0:NE, tt * 128:(tt + 1) * 128]),
                 r=("Brow", "const"), w=("pC",))
            p.op("vector", CPY(a_stt[:, 40:40 + NE], psC[:, 384:384 + NE]), r=("pC",), w=("mk0",))
            mpc = sb_mpt[:, t0 // 128 + tt:t0 // 128 + tt + 1]
            p.op("vector", RMAX(mpc, a_stt[:, 40:40 + NE]), r=("mk0",), w=("mpt",))
            p.op("vector", TS(a_stt[:, 48:48 + NE], a_stt[:, 40:40 + NE], mpc, Op.is_ge),
                 r=("mk0", "mpt"), w=("mk1",))
            p.op("vector", CUMSUM(a_stt[:, 56:56 + NE], a_stt[:, 48:48 + NE], sb_z8[:, 0:NE], 0.0),
                 r=("mk1", "z8"), w=("mk2",))
            p.op("vector", TS(a_stt[:, 56:56 + NE], a_stt[:, 56:56 + NE], 1.0, Op.is_equal),
                 r=("mk2",), w=("mk2",))
            p.op("vector", TT(a_stt[:, 48:48 + NE], a_stt[:, 48:48 + NE], a_stt[:, 56:56 + NE],
                              Op.mult), r=("mk1", "mk2"), w=("mk1",))
            p.op("tensor", TRA(psC[0:NE, 0:128], a_stt[:, 48:48 + NE]), r=("mk1", "const"), w=("pC",))
            p.op("vector", CPY(R32(sb_msk[:, t0 + tt * 128:t0 + (tt + 1) * 128]), psC[0:NE, 0:128]),
                 r=("pC",), w=("msk",))

    if debug:
        p.op("gpsimd", lambda e: e.dma_start(out=dbg["d_msk"][:], in_=sb_msk[:]),
             r=("msk",), dma=True, dmasem="dbg")
        p.op("gpsimd", lambda e: e.dma_start(out=dbg["d_mp"][:], in_=sb_mpt[:]),
             r=("mpt",), dma=True, dmasem="dbg")

    p.barrier()
    s1.close()

    s3 = ExitStack()
    a_xn1 = sb("p3_xn1", (128, NDM * T), s3)
    a_q = sb("p3_q", (128, T), s3)
    a_k = sb("p3_k", (128, T), s3)
    a_v = sb("p3_v", (128, T), s3)
    a_vt = sb("p3_vt", (128, 2 * (T // 128) * (HD + 1)), s3)
    a_pt = sb("p3_pt", (128, 2 * TC), s3)
    a_cx = sb("p3_cx", (HD + 1, TC), s3)
    a_ct = sb("p3_ct", (128, HD), s3)
    a_wq3 = sb("p3_wq", (128, 3 * DM), s3)
    a_sm3 = sb("p3_sm", (128, 16), s3)
    a_nm = sb("p3_nm", (1, 1), s3)
    WCS = 6
    a_wcs = sb("p3_wcs", (128, WCS * DM), s3)
    a_wcb = sb("p3_wcb", (128, WCS * DM), s3, BF16)

    wc_jobs = []
    for e_ in range(NE):
        for c in range(NDM):
            for s_ in range(4):
                wc_jobs.append(("w1", e_, c, s_))
        for r_ in range(32):
            wc_jobs.append(("w2", e_, r_))
    _wc_i = [0]
    _wc_pend = []

    def _wc_aps(k):
        jb = wc_jobs[k]
        sl = k % WCS
        stg = a_wcs[:, sl * DM:(sl + 1) * DM]
        stb = a_wcb[:, sl * DM:(sl + 1) * DM]
        if jb[0] == "w1":
            _, e_, c, s_ = jb
            src = W1_h[e_, c * 128:(c + 1) * 128, s_ * DM:(s_ + 1) * DM]
            dst = s_W1b[e_].rearrange("q (t jj) -> q t jj", jj=DM)[
                :, s_ * 8:(s_ + 1) * 8, c * 128:(c + 1) * 128]
            dres = "W1bD"
        else:
            _, e_, r_ = jb
            qt, c = r_ // 8, r_ % 8
            src = W2_h[e_, r_ * 128:(r_ + 1) * 128, :]
            dst = s_W2b[e_].rearrange("q (t jj) -> q t jj", jj=4 * DM)[
                :, :, qt * DM + c * 128:qt * DM + (c + 1) * 128]
            dres = "W2bD"
        return sl, stg, stb, src, dst, dres

    def _wc_cs(k):
        sl, stg, stb, src, dst, dres = _wc_aps(k)
        p.op("vector", CPY(stb, stg), r=(f"wcs{sl}",), w=(f"wcb{sl}",))
        p.op("scalar", lambda e, dst=dst, stb=stb: e.dma_start(
            out=dst, in_=stb.rearrange("q (t j) -> q t j", j=128)),
            r=(f"wcb{sl}",), w=(dres,), dma=True, dmasem="WcS")

    def emit_wc(n):
        for _ in range(n):
            if _wc_i[0] >= len(wc_jobs):
                break
            k = _wc_i[0]
            _wc_i[0] += 1
            sl, stg, stb, src, dst, dres = _wc_aps(k)
            p.op("sync", lambda e, stg=stg, src=src: e.dma_start(out=stg, in_=src),
                 w=(f"wcs{sl}",), dma=True, dmasem="WcL")
            _wc_pend.append(k)
            if len(_wc_pend) == WCS:
                _wc_cs(_wc_pend.pop(0))

    def emit_wc_flush():
        while _wc_pend:
            _wc_cs(_wc_pend.pop(0))

    for dm in range(NDM):
        p.op("sync", lambda e, dm=dm: e.dma_start(
            out=a_xn1[:, dm * T:(dm + 1) * T].bitcast(F32R), in_=s_xn1[dm].bitcast(F32R)),
            r=(f"s_xn1{dm}",), w=(f"xn1r{dm}",), dma=True, dmasem="Lxn1")

    for o in range(NDM):
        for wi, (skey, sW) in enumerate((("SWq", s_WqT), ("SWk", s_WkT), ("SWv", s_WvT))):
            p.op("sync", lambda e, o=o, wi=wi, sW=sW: e.dma_start(
                out=a_wq3[:, wi * DM:(wi + 1) * DM].bitcast(F32R), in_=sW[o].bitcast(F32R)),
                r=(skey,), w=(f"wq3_{wi}",), dma=True, dmasem="Lwq")
        for wi, (dstb, dres, bcol) in enumerate(
                ((a_q, "q3", sb_bqkv[:, o:o + 1]),
                 (a_k, "k3", sb_bqkv[:, NDM + o:NDM + o + 1]),
                 (a_v, "v3", sb_bqkv[:, 2 * NDM + o:2 * NDM + o + 1]))):
            for tc4 in range(T // TC):
                par = tc4 % 2
                pa = psA[:, par * 512:par * 512 + TC]
                for dm in range(NDM):
                    p.op("tensor", MM(pa, a_wq3[:, wi * DM + dm * 128:wi * DM + (dm + 1) * 128],
                                      a_xn1[:, dm * T + tc4 * TC:dm * T + (tc4 + 1) * TC],
                                      dm == 0, dm == NDM - 1),
                         r=(f"wq3_{wi}",) + tuple(f"xn1r{d}" for d in range(NDM)),
                         w=(f"pA{par}",))
                p.op("scalar", ACT(R32(dstb[:, tc4 * TC:(tc4 + 1) * TC]), pa, AF.Identity, bias=bcol),
                     r=(f"pA{par}", "const"), w=(dres,))
        if debug:
            p.op("gpsimd", lambda e, o=o: e.dma_start(out=dbg["d_q"][o], in_=a_q[:]),
                 r=("q3",), dma=True, dmasem="dbg")
            p.op("gpsimd", lambda e, o=o: e.dma_start(out=dbg["d_k"][o], in_=a_k[:]),
                 r=("k3",), dma=True, dmasem="dbg")
            p.op("gpsimd", lambda e, o=o: e.dma_start(out=dbg["d_v"][o], in_=a_v[:]),
                 r=("v3",), dma=True, dmasem="dbg")
        for wi, (srcb, sres, col) in enumerate(((a_q, "q3", 0), (a_k, "k3", 1))):
            p.op("scalar", ACT(R32(a_vt[:, 0:T]), srcb[:], AF.Square), r=(sres,), w=("vt",))
            p.op("vector", MEMS(a_sm3[:, col:col + 1], 0.0), w=(f"bnd{col}",))
            for tc4 in range(T // TC):
                p.op("tensor", MM(psD[0:1, 0:TC], sb_onc[:], a_vt[:, tc4 * TC:(tc4 + 1) * TC],
                                  True, True), r=("vt", "const"), w=("pD",))
                p.op("vector", RMAX(a_sm3[0:1, 4 + col:5 + col], psD[0:1, 0:TC]),
                     r=("pD",), w=(f"bnd{col}",))
                p.op("vector", TT(a_sm3[0:1, col:col + 1], a_sm3[0:1, col:col + 1],
                                  a_sm3[0:1, 4 + col:5 + col], Op.max),
                     r=(f"bnd{col}",), w=(f"bnd{col}",))
        p.op("scalar", ACT(a_sm3[0:1, 2:3], a_sm3[0:1, 0:1], AF.Sqrt), r=("bnd0",), w=("bndm",))
        p.op("scalar", ACT(a_sm3[0:1, 3:4], a_sm3[0:1, 1:2], AF.Sqrt), r=("bnd1",), w=("bndm",))
        p.op("vector", TT(a_sm3[0:1, 2:3], a_sm3[0:1, 2:3], a_sm3[0:1, 3:4], Op.mult),
             r=("bndm",), w=("bndm",))
        p.op("vector", TS(R32(a_nm[0:1, 0:1]), a_sm3[0:1, 2:3], 0.125, Op.mult, -1.0, Op.mult),
             r=("bndm", "const"), w=("bndm",))
        p.op("tensor", MMF32(psD[:, 0:1], sb_on[:], a_nm[0:1, 0:1], True, True),
             r=("bndm", "const"), w=("pD",))
        p.op("vector", CPY(a_sm3[:, 6:7], psD[:, 0:1]), r=("pD",), w=("negm",))
        NEGM = a_sm3[:, 6:7]
        p.op("vector", MEMS(a_vt[:], 1.0), r=("v3",), w=("vt",))
        for hh in range(2):
            for tk in range(T // 128):
                slot = tk % 4
                p.op("tensor", TRA(psC[:, slot * 128:slot * 128 + HD],
                                   a_v[hh * HD:(hh + 1) * HD, tk * 128:(tk + 1) * 128]),
                     r=("v3", "const"), w=("pC",))
                base = (hh * (T // 128) + tk) * (HD + 1)
                p.op("vector", CPY(R32(a_vt[:, base:base + HD]), psC[:, slot * 128:slot * 128 + HD]),
                     r=("pC",), w=("vt",))
        for hh in range(2):
            h = 2 * o + hh
            for qc in range(T // TC):
                for tk in range(T // 128):
                    par = tk % 2
                    ps_s = psB[:, par * 512:par * 512 + TC]
                    p.op("tensor", MM(ps_s, a_k[hh * HD:(hh + 1) * HD, tk * 128:(tk + 1) * 128],
                                      a_q[hh * HD:(hh + 1) * HD, qc * TC:(qc + 1) * TC],
                                      True, True), r=("k3", "q3"), w=(f"pB{par}",))
                    p.op("scalar", ACT(R32(a_pt[:, par * TC:(par + 1) * TC]), ps_s, AF.Exp,
                                       scale=C(0.125), bias=NEGM),
                         r=(f"pB{par}", "negm", "const"), w=(f"pt{par}",))
                    base = (hh * (T // 128) + tk) * (HD + 1)
                    p.op("tensor", MM(psF[0:HD + 1, 0:TC], a_vt[:, base:base + HD + 1],
                                      a_pt[:, par * TC:(par + 1) * TC],
                                      tk == 0, tk == T // 128 - 1),
                         r=(f"pt{par}", "vt"), w=("pF",))
                p.op("scalar", ACT(a_cx[:], psF[0:HD + 1, 0:TC], AF.Identity),
                     r=("pF",), w=("cx",))
                for tt in range(TC // 128):
                    p.op("tensor", TRA(psC[:, 0:HD + 1], a_cx[:, tt * 128:(tt + 1) * 128]),
                         r=("cx", "const"), w=("pC",))
                    p.op("vector", REC(a_sm3[:, 7:8], psC[:, HD:HD + 1]), r=("pC",), w=("rden",))
                    p.op("vector", TS(a_ct[:], psC[:, 0:HD], a_sm3[:, 7:8], Op.mult),
                         r=("pC", "rden"), w=("ct",))
                    ti = qc * (TC // 128) + tt
                    p.op("gpsimd", lambda e, ti=ti, h=h: e.dma_start(
                        out=s_ctxt[ti, :, h * HD:(h + 1) * HD], in_=a_ct[:]),
                        r=("ct",), w=(f"s_ct{ti}",), dma=True, dmasem="Sct")
        emit_wc(64)
    emit_wc_flush()

    p.barrier()
    s3.close()

    s4 = ExitStack()
    a_xn4 = sb("p4_xn1", (128, NDM * T), s4)
    a_h1 = sb("p4_h1", (128, 2 * NFN * TC), s4, BF16)
    a_xe = sb("p4_xe", (128, 2 * NDM * TC), s4, BF16)
    a_w14b = sb("p4_w1b", (128, 2 * DM), s4, BF16)
    a_w24b = sb("p4_w2b", (128, 2 * DM), s4, BF16)
    a_mo = sb("p4_mo", (128, 2 * TC), s4)

    for dm in range(NDM):
        p.op("sync", lambda e, dm=dm: e.dma_start(
            out=a_xn4[:, dm * T:(dm + 1) * T].bitcast(F32R), in_=s_xn1[dm].bitcast(F32R)),
            r=(f"s_xn1{dm}",), w=(f"xn4_{dm}",), dma=True, dmasem="Lxn1")

    for e_ in range(NE):
        for tcp in range(T // TC // 2):
            for j in range(2):
                tc4 = tcp * 2 + j
                p.op("tensor", MM(psD[:, 0:TC], sb_sel[0:NE, e_ * 128:(e_ + 1) * 128],
                                  sb_msk[0:NE, tc4 * TC:(tc4 + 1) * TC],
                                  True, True), r=("msk", "const"), w=("pD",))
                for dm in range(NDM):
                    p.op("vector", STT(a_xe[:, (j * NDM + dm) * TC:(j * NDM + dm + 1) * TC],
                                       a_xn4[:, dm * T + tc4 * TC:dm * T + (tc4 + 1) * TC],
                                       0.0, psD[:, 0:TC], Op.bypass, Op.mult),
                         r=(f"xn4_{dm}", "pD"), w=(f"xe{j}",))
            for f in range(NFN):
                fp = f % 2
                blkb = a_w14b[:, fp * DM:(fp + 1) * DM]
                p.op("sync", lambda e, e_=e_, f=f, blkb=blkb: e.dma_start(
                    out=blkb, in_=s_W1b[e_][:, f * DM:(f + 1) * DM]),
                    r=("W1bD",), w=(f"w14b{fp}",), dma=True, dmasem=f"w14{fp}")
                paJ = (psA[:, fp * 512:fp * 512 + TC],
                       (psE if fp == 0 else psF)[:, 0:TC])
                prJ = (f"pA{fp}", "pE" if fp == 0 else "pF")
                for dm in range(NDM):
                    for j in range(2):
                        p.op("tensor", MMB(paJ[j], blkb[:, dm * 128:(dm + 1) * 128],
                                           a_xe[:, (j * NDM + dm) * TC:(j * NDM + dm + 1) * TC],
                                           dm == 0, dm == NDM - 1),
                             r=(f"w14b{fp}", f"xe{j}"), w=(prJ[j],),
                             nosig=(dm != NDM - 1))
                for j in range(2):
                    p.op("scalar", ACT(a_h1[:, (j * NFN + f) * TC:(j * NFN + f + 1) * TC],
                                       paJ[j], AF.Gelu), r=(prJ[j],), w=("h1",))
            for dm in range(NDM):
                for qt in range(4):
                    hp = qt % 2
                    blk2b = a_w24b[:, hp * DM:(hp + 1) * DM]
                    p.op("sync", lambda e, e_=e_, dm=dm, qt=qt, blk2b=blk2b: e.dma_start(
                        out=blk2b, in_=s_W2b[e_][:, (dm * 4 + qt) * DM:(dm * 4 + qt + 1) * DM]),
                        r=("W2bD",), w=(f"w24b{hp}",), dma=True, dmasem=f"w24{hp}")
                    for fh in range(NFN // 4):
                        f = qt * (NFN // 4) + fh
                        for j in range(2):
                            p.op("tensor", MMB(psB[:, j * 512:j * 512 + TC],
                                               blk2b[:, fh * 128:(fh + 1) * 128],
                                               a_h1[:, (j * NFN + f) * TC:(j * NFN + f + 1) * TC],
                                               f == 0, f == NFN - 1),
                                 r=(f"w24b{hp}", "h1"), w=(f"pB{j}",),
                                 nosig=not (fh == NFN // 4 - 1
                                            and (j == 1 or qt == 3)))
                for j in range(2):
                    p.op("scalar", ACT(a_mo[:, j * TC:(j + 1) * TC], psB[:, j * 512:j * 512 + TC],
                                       AF.Identity), r=(f"pB{j}",), w=(f"mo{j}",))
                    tc4 = tcp * 2 + j
                    p.op("gpsimd", lambda e, e_=e_, dm=dm, tc4=tc4, j=j: e.dma_start(
                        out=s_moe[dm, :, tc4 * TC:(tc4 + 1) * TC], in_=a_mo[:, j * TC:(j + 1) * TC],
                        accum_op=(Op.add if e_ > 0 else Op.bypass)),
                        r=(f"mo{j}",), w=(f"s_mo{dm}_{tc4}",), dma=True, dmasem="Smoe",
                        swdge=True)

    p.barrier()
    s4.close()

    s5 = ExitStack()
    a_x1f = sb("p5_x1", (128, NDM * T), s5)
    a_mof = sb("p5_mo", (128, NDM * T), s5)
    a_ctf = sb("p5_ct", (128, 2 * DM), s5)
    a_mot = sb("p5_mt", (128, 2 * 512), s5)
    a_of = sb("p5_of", (128, 2 * DM), s5)

    for dm in range(NDM):
        p.op("sync", lambda e, dm=dm: e.dma_start(
            out=a_x1f[:, dm * T:(dm + 1) * T], in_=s_x1[dm]),
            r=(f"s_x1{dm}",), w=(f"x1f_{dm}",), dma=True, dmasem="Lx1f")
        p.op("sync", lambda e, dm=dm: e.dma_start(
            out=a_mof[:, dm * T:(dm + 1) * T], in_=s_moe[dm]),
            r=tuple(f"s_mo{dm}_{t4}" for t4 in range(T // TC)),
            w=(f"mof_{dm}",), dma=True, dmasem="Lmof")
    if debug:
        for dm in range(NDM):
            p.op("gpsimd", lambda e, dm=dm: e.dma_start(
                out=dbg["d_moe"][dm], in_=a_mof[:, dm * T:(dm + 1) * T]),
                r=(f"mof_{dm}",), dma=True, dmasem="dbg")

    for ft in range(T // 128):
        par = ft % 2
        p.op("gpsimd", lambda e, ft=ft, par=par: e.dma_start(
            out=a_ctf[:, par * DM:(par + 1) * DM].bitcast(F32R), in_=s_ctxt[ft].bitcast(F32R)),
            r=(f"s_ct{ft}",), w=(f"ctf{par}",), dma=True, dmasem=f"Lct{par}")
        if debug:
            p.op("gpsimd", lambda e, ft=ft, par=par: e.dma_start(
                out=dbg["d_ctx"][ft], in_=a_ctf[:, par * DM:(par + 1) * DM]),
                r=(f"ctf{par}",), dma=True, dmasem="dbg")
        for g in range(2):
            pa = psA[:, par * 512:(par + 1) * 512]
            for j in range(4):
                dm = g * 4 + j
                p.op("tensor", TRA(pa[:, j * 128:(j + 1) * 128],
                                   a_mof[:, dm * T + ft * 128:dm * T + (ft + 1) * 128],
                                   j == 0, j == 3),
                     r=(f"mof_{dm}", "const"), w=(f"pA{par}",))
            p.op("vector", TS(R32(a_mot[:, par * 512:(par + 1) * 512]), pa, sb_mpt[:, ft:ft + 1],
                              Op.mult), r=(f"pA{par}", "mpt"), w=(f"mot{par}",))
            pb = psB[:, par * 512:(par + 1) * 512]
            for j in range(4):
                dm = g * 4 + j
                p.op("tensor", TRA(pb[:, j * 128:(j + 1) * 128],
                                   a_x1f[:, dm * T + ft * 128:dm * T + (ft + 1) * 128],
                                   j == 0, False),
                     r=(f"x1f_{dm}", "const"), w=(f"pB{par}",))
            p.op("tensor", MM(pb, sb_id[:], a_mot[:, par * 512:(par + 1) * 512], False, False),
                 r=(f"mot{par}", "const"), w=(f"pB{par}",))
            p.op("tensor", MM(pb, sb_id[:], a_ctf[:, par * DM + g * 512:par * DM + (g + 1) * 512],
                              False, True), r=(f"ctf{par}", "const"), w=(f"pB{par}",))
            p.op("scalar", ACT(a_of[:, par * DM + g * 512:par * DM + (g + 1) * 512], pb,
                               AF.Identity), r=(f"pB{par}",), w=(f"of{par}",))
        p.op("gpsimd", lambda e, ft=ft, par=par: e.dma_start(
            out=out_h[ft * 128:(ft + 1) * 128, :], in_=a_of[:, par * DM:(par + 1) * DM]),
            r=(f"of{par}",), w=("outd",), dma=True, dmasem="Sout")
    p.op("gpsimd", lambda e: e.nop(), r=("outd",))
    s5.close()

    if max_ops is not None:
        keep = p.all[:max_ops]
        p.all = keep
        kept = set(id(o) for o in keep)
        for e in ENGS:
            p.per_eng[e] = [o for o in p.per_eng[e] if id(o) in kept]
        for i, o in enumerate(p.all):
            o.i = i
        p.barrier()
    if mutate is not None:
        mutate(p)
    nsem, nsig = p.plan(es, sim_safe=sim_safe)
    with nc.allow_low_precision(reason="fp32r (fp22) operand production feeding fp32r matmuls"):
        with nc.Block() as block:
            p.emit(block)
    if debug:
        print(f"[build] ops={len(p.all)} sems={nsem} signaled={nsig}")
    return nc


_NC_CACHE = {}


def kernel(**inputs):
    if "nc" not in _NC_CACHE:
        _NC_CACHE["nc"] = build_program(sim_safe=True)
    nc = _NC_CACHE["nc"]
    f32 = lambda a: np.ascontiguousarray(np.asarray(a), dtype=np.float32)
    x = f32(inputs["L0_attn_x"])
    common = {
        "x_full": x.reshape(B * T, DM),
        "ln_gamma": f32(inputs["L0_attn_ln_gamma"]),
        "ln_beta": f32(inputs["L0_attn_ln_beta"]),
        "W_in": f32(inputs["L0_attn_W_in"]),
        "conv_w": f32(inputs["L0_attn_conv_w"]),
        "conv_b": f32(inputs["L0_attn_conv_b"]),
        "W_x": f32(inputs["L0_attn_W_x"]),
        "w_dt": f32(inputs["L0_attn_w_dt"]),
        "b_dt": f32(inputs["L0_attn_b_dt"]),
        "A_log": f32(inputs["L0_attn_A_log"]),
        "D_param": f32(inputs["L0_attn_D_param"]),
        "W_out": f32(inputs["L0_attn_W_out"]),
        "ffn_w": f32(inputs["L0_ffn_weight"]),
        "ffn_b": f32(inputs["L0_ffn_bias"]),
        "Wq": f32(inputs["L1_attn_Wq"]),
        "bq": f32(inputs["L1_attn_bq"]),
        "Wk": f32(inputs["L1_attn_Wk"]),
        "bk": f32(inputs["L1_attn_bk"]),
        "Wv": f32(inputs["L1_attn_Wv"]),
        "bv": f32(inputs["L1_attn_bv"]),
        "Wr": f32(inputs["L1_ffn_Wr"]),
        "br": f32(inputs["L1_ffn_br"]),
        "W1": f32(inputs["L1_ffn_W1"]),
        "W2": f32(inputs["L1_ffn_W2"]),
    }
    in_maps = []
    for c in range(B):
        m = dict(common)
        m["x_own"] = x[c]
        in_maps.append(m)
    res = run_bass_kernel_spmd(nc, in_maps, core_ids=list(range(B)))
    out = np.stack([np.asarray(r["out"]) for r in res.results], axis=0)
    return out.astype(np.float32)
